# version 08
# baseline (speedup 1.0000x reference)
import math
from contextlib import ExitStack

import numpy as np
import ml_dtypes

import concourse.bass as bass
import concourse.mybir as mybir
from concourse.bass_utils import run_bass_kernel_spmd

F32 = mybir.dt.float32
BF16 = mybir.dt.bfloat16
AF = mybir.ActivationFunctionType
ALU = mybir.AluOpType
AX = mybir.AxisListType

N_CORES = 8
D = 1024
NH, DH, WIN = 16, 64, 256
I_FF, NE = 2048, 8
DI, NSTATE, DTR, KCONV = 2048, 16, 64, 4
M_LEAF = 256
LN_EPS = 1e-5
RMS_EPS = 1.1920928955078125e-07
CSHIFT = 24.0


class Cfg:
    def __init__(self, S=2, T=2048, slab=256, ttb=1024, tc=256, ttd=1024):
        self.S, self.T, self.SLAB, self.TTB, self.TC, self.TTD = S, T, slab, ttb, tc, ttd
        assert T % slab == 0 and slab % 128 == 0 and slab >= 256
        assert T % ttb == 0 and T % tc == 0 and T % ttd == 0 and tc % 128 == 0



def make_consts(cfg):
    T = cfg.T
    inv = 1.0 / 10000.0 ** (np.arange(0, DH, 2) / DH)
    fr = np.outer(np.arange(T), inv)
    cos, sin = np.cos(fr), np.sin(fr)
    p = np.arange(128)
    j = p % 64
    jj = j % 32
    sgn = np.where(j < 32, -1.0, 1.0)
    cos_pt = cos[:, jj].T
    sin_pt = (sin[:, jj] * sgn[None, :]).T
    c = {}
    c["costab_q"] = np.ascontiguousarray(cos_pt / math.sqrt(DH)).astype(np.float32)
    c["sintab_q"] = np.ascontiguousarray(sin_pt / math.sqrt(DH)).astype(np.float32)
    c["costab_k"] = np.ascontiguousarray(cos_pt).astype(np.float32)
    c["sintab_k"] = np.ascontiguousarray(sin_pt).astype(np.float32)
    q = np.arange(128)[None, :]
    k = np.arange(128)[:, None]
    c["maskT"] = np.ascontiguousarray(np.stack(
        [(q < k), np.ones((128, 128), bool), (k <= q)], 1)).astype(np.float32)
    c["ident"] = np.eye(128, dtype=np.float32)
    c["ones_col"] = np.ones((128, 1), np.float32)
    c["ones_row"] = np.ones((1, 128), np.float32)
    c["cshift_col"] = np.full((128, 1), -CSHIFT, np.float32)
    c["eps_ln"] = np.full((1, 1), LN_EPS, np.float32)
    c["eps_rms"] = np.full((1, 1), RMS_EPS, np.float32)
    sel8 = np.zeros((8, 8, 128), np.float32)
    for e in range(8):
        sel8[e, e, :] = 1.0
    c["sel8"] = sel8
    esel = np.zeros((32, 32, 128), np.float32)
    for n in range(32):
        esel[n, n, :] = 1.0
    c["esel"] = esel
    c["esel_cols"] = np.eye(32, dtype=np.float32)
    mats, index = [], []
    for L in range(8):
        for lpb in range(2):
            byg = {}
            for mloc in range(128):
                mg = lpb * 128 + mloc
                node = (2 ** L - 1) + (mg >> (8 - L))
                gpb, r = node // 128, node % 128
                byg.setdefault(gpb, np.zeros((128, 128), np.float32))[r, mloc] = 1.0
            for gpb, m in sorted(byg.items()):
                index.append((L, lpb, gpb, len(mats)))
                mats.append(m)
    c["psel"] = np.ascontiguousarray(np.stack(mats, 1))
    PSEL_INDEX = index
    c["_psel_index"] = index
    sgn_t = np.zeros((128, 2, 8), np.float32)
    bia_t = np.zeros((128, 2, 8), np.float32)
    for L in range(8):
        for lpb in range(2):
            for mloc in range(128):
                b = ((lpb * 128 + mloc) >> (7 - L)) & 1
                sgn_t[mloc, lpb, L] = 2.0 * b - 1.0
                bia_t[mloc, lpb, L] = 1.0 - b
    c["sgn_tab"] = sgn_t
    c["bias_tab"] = bia_t
    return c


def prep_weights(inp):
    A = np.ascontiguousarray
    w = {}
    wqkv = inp["L0_attn_w_qkv"]
    w["WqT"], w["WkT"], w["WvT"] = (A(wqkv[i * D:(i + 1) * D].T) for i in range(3))
    w["WoT"] = A(inp["L0_attn_w_out"].T)
    w["WsgT"] = A(inp["L0_ffn_sg_w"].T)
    w["WsuT"] = A(inp["L0_ffn_su_w"].T)
    w["WsdT"] = A(inp["L0_ffn_sd_w"].T)
    w["routerT"] = A(inp["L0_ffn_router_w"].T)
    w["router_b"] = A(inp["L0_ffn_routing_bias"].reshape(8, 1))
    w["WgT"] = A(inp["L0_ffn_wg"].transpose(0, 2, 1))
    w["WuT"] = A(inp["L0_ffn_wu"].transpose(0, 2, 1))
    w["WdT"] = A(inp["L0_ffn_wd"].transpose(0, 2, 1))
    w["normw_fm"] = A(inp["L1_attn_norm_w"].reshape(16, 64).reshape(8, 128).T)
    w["normw_fm"] = A(inp["L1_attn_norm_w"].reshape(8, 128).T)
    w["WipT"] = A(inp["L1_attn_in_proj_w"].T)
    w["convw_fm"] = A(inp["L1_attn_conv_w"].reshape(16, 128, KCONV).transpose(1, 0, 2))
    w["convb_fm"] = A(inp["L1_attn_conv_b"].reshape(16, 128).T)
    w["WxpT"] = A(inp["L1_attn_x_proj_w"].T)
    w["WdtT"] = A(inp["L1_attn_dt_proj_w"].T)
    w["dtb_fm"] = A(inp["L1_attn_dt_proj_b"].reshape(16, 128).T)
    w["Alog_fm"] = A(inp["L1_attn_A_log"].reshape(16, 128, NSTATE).transpose(1, 0, 2))
    w["D_fm"] = A(inp["L1_attn_D"].reshape(16, 128).T)
    w["WopT"] = A(inp["L1_attn_out_proj_w"].T)
    w["Wq2T"] = A(inp["L1_ffn_Wq"].T)
    w["routersT"] = A(np.pad(inp["L1_ffn_routers"].T, ((0, 0), (0, 1))))
    w["LV"] = A(inp["L1_ffn_leaf_values"])
    w["Wout2T"] = A(inp["L1_ffn_Wout_w"].T)
    w["Woutb_fm"] = A(inp["L1_ffn_Wout_b"].reshape(8, 128).T)
    return w



SEMS = ("LD", "LG", "PE", "VE", "SE", "ST", "PEa", "PEb", "VEa", "VEb", "GP")


class PB:
    def __init__(self, nc, stack, block):
        self.nc, self.stack, self.block = nc, stack, block
        self.sem, self.cnt = {}, {}
        for name in SEMS:
            self.sem[name] = stack.enter_context(nc.semaphore(name))
            self.cnt[name] = 0

    def bump(self, name, by=1):
        self.cnt[name] += by
        return self.cnt[name]

    def reserve(self, name, k):
        base = self.cnt[name]
        self.cnt[name] += k
        return base

    def snap(self):
        return dict(self.cnt)

    def barrier(self, eng, snap):
        for name, v in snap.items():
            if v > 0:
                eng.wait_ge(self.sem[name], v)


def sb(st, nc, name, shape, dtype):
    return st.enter_context(nc.sbuf_tensor(name, shape, dtype))


def ps(st, nc, name, shape, dtype=F32):
    return st.enter_context(nc.psum_tensor(name, shape, dtype))


def r128(ap):
    return ap.rearrange("(c p) n -> p c n", p=128)


PHASE_EMIT = {}


def emit_ln(pb, ctx, xbuf, nb, W, pre=(), eps=LN_EPS):
    nc, block, sem = pb.nc, pb.block, pb.sem
    onec, oner = ctx["onec"], ctx["oner"]
    psA, psB = ctx["psA"], ctx["psB"]
    t1, t2 = ctx["t1"], ctx["t2"]
    r0, r1, r2 = ctx["r0"], ctx["r1"], ctx["r2"]
    ndim = nb * 128

    se0 = pb.reserve("SE", nb)
    pe0 = pb.reserve("PE", nb)

    @block.scalar
    def _(s):
        for nm, v in pre:
            s.wait_ge(sem[nm], v)
        s.wait_ge(sem["PE"], pe0)
        for cb in range(nb):
            if cb >= 2:
                s.wait_ge(sem["PE"], pe0 + cb - 1)
            s.activation((t1 if cb % 2 == 0 else t2)[:, 0:W], xbuf[:, cb, 0:W],
                         AF.Square).then_inc(sem["SE"], 1)

    @block.tensor
    def _(tn):
        for nm, v in pre:
            tn.wait_ge(sem[nm], v)
        for cb in range(nb):
            tn.matmul(psA[0:1, 0:W], onec[:, 0:1], xbuf[:, cb, 0:W],
                      start=(cb == 0), stop=(cb == nb - 1))
        for cb in range(nb):
            tn.wait_ge(sem["SE"], se0 + cb + 1)
            tn.matmul(psB[0:1, 0:W], onec[:, 0:1],
                      (t1 if cb % 2 == 0 else t2)[:, 0:W],
                      start=(cb == 0), stop=(cb == nb - 1)).then_inc(sem["PE"], 1)
    pe_stats = pe0 + nb

    @block.vector
    def _(v):
        v.wait_ge(sem["PE"], pe_stats)
        v.tensor_scalar_mul(r0[:, 0:W], psA[0:1, 0:W], 1.0 / ndim).then_inc(sem["VE"], 1)
    ve_m = pb.bump("VE")

    @block.scalar
    def _(s):
        s.wait_ge(sem["VE"], ve_m)
        s.activation(r1[:, 0:W], r0[:, 0:W], AF.Square).then_inc(sem["SE"], 1)
    se_msq = pb.bump("SE")

    @block.vector
    def _(v):
        v.wait_ge(sem["SE"], se_msq)
        v.scalar_tensor_tensor(r2[:, 0:W], psB[0:1, 0:W], 1.0 / ndim,
                               r1[:, 0:W], ALU.mult, ALU.subtract).then_inc(sem["VE"], 1)
    ve_var = pb.bump("VE")

    se_ln0 = pb.reserve("SE", 2)

    @block.scalar
    def _(s):
        s.wait_ge(sem["VE"], ve_var)
        s.activation(r1[:, 0:W], r2[:, 0:W], AF.Ln,
                     bias=ctx["eps_ap"][:]).then_inc(sem["SE"], 1)
        s.wait_ge(sem["SE"], se_ln0 + 1)
        s.activation(r2[:, 0:W], r1[:, 0:W], AF.Exp,
                     scale=-0.5).then_inc(sem["SE"], 1)
    se_rstd = se_ln0 + 2

    @block.tensor
    def _(tn):
        tn.wait_ge(sem["SE"], se_rstd)
        tn.matmul(psA[:, 0:W], oner[0:1, :], r0[:, 0:W], start=True, stop=True)
        tn.matmul(psB[:, 0:W], oner[0:1, :], r2[:, 0:W],
                  start=True, stop=True).then_inc(sem["PE"], 1)
    pe_bc = pb.bump("PE")

    ve0 = pb.reserve("VE", 2 * nb)

    @block.vector
    def _(v):
        v.wait_ge(sem["PE"], pe_bc)
        for cb in range(nb):
            v.tensor_sub(xbuf[:, cb, 0:W], xbuf[:, cb, 0:W],
                         psA[:, 0:W]).then_inc(sem["VE"], 1)
            v.wait_ge(sem["VE"], ve0 + 2 * cb + 1)
            v.tensor_mul(xbuf[:, cb, 0:W], xbuf[:, cb, 0:W],
                         psB[:, 0:W]).then_inc(sem["VE"], 1)
    return ve0 + 2 * nb



def emit_phaseA(pb, cfg, hbm, xin_name, xout_name):
    nc, block, sem = pb.nc, pb.block, pb.sem
    S, T, SLAB = cfg.S, cfg.T, cfg.SLAB
    assert 256 <= SLAB <= 512
    QTS = SLAB // 128
    NSLAB = T // SLAB
    start = pb.snap()

    with ExitStack() as st:
        XS = sb(st, nc, "aXS", [128, 8, SLAB], F32)
        XSb = sb(st, nc, "aXSb", [128, 8, SLAB], BF16)
        Qf = sb(st, nc, "aQf", [128, 8, SLAB], F32)
        Qs = sb(st, nc, "aQs", [128, 8, SLAB], F32)
        qrot = sb(st, nc, "aqrot", [128, 8, SLAB], BF16)
        krot = sb(st, nc, "akrot", [128, 2, 8, SLAB], BF16)
        vsb = sb(st, nc, "avsb", [128, 2, QTS, NH, 66], BF16)
        PTraw = sb(st, nc, "aPTraw", [128, 2, 3, 128], BF16)
        PTmsk = sb(st, nc, "aPTmsk", [128, 2, 3, 128], BF16)
        att = sb(st, nc, "aatt", [128, NH * DH], BF16)
        afm = sb(st, nc, "aafm", [128, 8, SLAB], BF16)
        xo = sb(st, nc, "axo", [128, 8, SLAB], F32)
        rinv = sb(st, nc, "arinv", [128, 2], F32)
        t1 = sb(st, nc, "at1", [128, 512], F32)
        t2 = sb(st, nc, "at2", [128, 512], F32)
        rt1 = sb(st, nc, "art1", [128, SLAB], F32)
        rt2 = sb(st, nc, "art2", [128, SLAB], F32)
        rows = sb(st, nc, "arows", [1, 3, 512], F32)
        cosq = sb(st, nc, "acosq", [128, SLAB], F32)
        sinq = sb(st, nc, "asinq", [128, SLAB], F32)
        cosk = sb(st, nc, "acosk", [128, SLAB], F32)
        sink = sb(st, nc, "asink", [128, SLAB], F32)
        maskT = sb(st, nc, "amaskT", [128, 3, 128], BF16)
        identb = sb(st, nc, "aidentb", [128, 128], BF16)
        onec = sb(st, nc, "aonec", [128, 1], F32)
        oner = sb(st, nc, "aoner", [1, 128], F32)
        cshift = sb(st, nc, "acshift", [128, 1], F32)
        epsln = sb(st, nc, "aepsln", [1, 1], F32)
        WqTs = sb(st, nc, "aWqT", [128, 8, D], BF16)
        WkTs = sb(st, nc, "aWkT", [128, 8, D], BF16)
        WvTs = sb(st, nc, "aWvT", [128, 8, D], BF16)
        WoTs = sb(st, nc, "aWoT", [128, 8, D], BF16)
        psE = [ps(st, nc, f"apsE{i}", [128, 512]) for i in range(2)]
        psS = [ps(st, nc, f"apsS{i}", [128, 3, 128]) for i in range(2)]
        psO = [ps(st, nc, f"apsO{i}", [128, 66]) for i in range(2)]
        psTR = [ps(st, nc, f"apsTR{i}", [128, 128], BF16) for i in range(2)]

        ln_ctx = dict(onec=onec, oner=oner, psA=psE[0], psB=psE[1],
                      t1=t1, t2=t2, r0=rows[:, 0, :], r1=rows[:, 1, :],
                      r2=rows[:, 2, :], eps_ap=epsln)

        @block.sync
        def _(sy):
            pb.barrier(sy, start)
            sy.dma_start(out=onec[:], in_=hbm["ones_col"]).then_inc(sem["LD"], 16)
            sy.dma_start(out=oner[:], in_=hbm["ones_row"]).then_inc(sem["LD"], 16)
            sy.dma_start(out=cshift[:], in_=hbm["cshift_col"]).then_inc(sem["LD"], 16)
            sy.dma_start(out=epsln[:], in_=hbm["eps_ln"]).then_inc(sem["LD"], 16)
        ld_consts = pb.bump("LD", 16 * 4)

        @block.gpsimd
        def _(g):
            pb.barrier(g, start)
            for nm, dst in (("WqT", WqTs), ("WkT", WkTs), ("WvT", WvTs), ("WoT", WoTs)):
                g.dma_start(out=dst[:], in_=r128(hbm[nm])).then_inc(sem["LG"], 16)
            g.dma_start(out=identb[:], in_=hbm["ident"]).then_inc(sem["LG"], 16)
            g.dma_start(out=maskT[:], in_=hbm["maskT"]).then_inc(sem["LG"], 16)
        lg_w = pb.bump("LG", 16 * 6)

        st_x1 = 0
        ve_xo_resid = 0
        ve_rope_k_prev = 0
        pe_v_last = 0
        ve_ln_prev = 0
        pe_tr_prev = None

        for s in range(S):
            for jjj in range(NSLAB):
                par = jjj % 2
                t0 = jjj * SLAB
                first = (s == 0 and jjj == 0)

                @block.sync
                def _(sy):
                    sy.wait_ge(sem["LD"], pb.cnt["LD"])
                    if ve_xo_resid:
                        sy.wait_ge(sem["VE"], ve_xo_resid)
                    if ve_rope_k_prev:
                        sy.wait_ge(sem["VE"], ve_rope_k_prev)
                    sy.dma_start(out=XS[:], in_=hbm[xin_name][s, :, t0:t0 + SLAB]
                                 .rearrange("(c p) n -> p c n", p=128)).then_inc(sem["LD"], 16)
                    for dst, nm in ((cosq, "costab_q"), (sinq, "sintab_q"),
                                    (cosk, "costab_k"), (sink, "sintab_k")):
                        sy.dma_start(out=dst[:], in_=hbm[nm][:, t0:t0 + SLAB]).then_inc(sem["LD"], 16)
                ld_slab = pb.bump("LD", 16 * 5)

                @block.vector
                def _(v):
                    v.wait_ge(sem["LD"], ld_slab)
                    if first:
                        v.wait_ge(sem["LG"], lg_w)
                        v.wait_ge(sem["LD"], ld_consts)
                    if pe_v_last:
                        v.wait_ge(sem["PE"], pe_v_last)
                    v.tensor_copy(XSb[:], XS[:]).then_inc(sem["VE"], 1)
                ve_xsb = pb.bump("VE")

                ve_rope = {}
                for which in ("q", "k"):
                    Wsrc = WqTs if which == "q" else WkTs
                    ctab = cosq if which == "q" else cosk
                    stab = sinq if which == "q" else sink
                    pe_qk = pb.reserve("PE", 8)
                    ve_ev = pb.reserve("VE", 8)

                    @block.tensor
                    def _(tn, Wsrc=Wsrc, pe_qk=pe_qk, ve_ev=ve_ev, which=which):
                        tn.wait_ge(sem["VE"], ve_xsb)
                        if which == "q" and ve_ln_prev:
                            tn.wait_ge(sem["VE"], ve_ln_prev)
                        if which == "k":
                            tn.wait_ge(sem["VE"], ve_rope["q"])
                        for ob in range(8):
                            if ob >= 2:
                                tn.wait_ge(sem["VE"], ve_ev + ob - 1)
                            for cb in range(8):
                                mi = tn.matmul(psE[ob % 2][:, 0:SLAB],
                                               Wsrc[:, cb, ob * 128:(ob + 1) * 128],
                                               XSb[:, cb, :],
                                               start=(cb == 0), stop=(cb == 7))
                            mi.then_inc(sem["PE"], 1)

                    @block.vector
                    def _(v, pe_qk=pe_qk, ve_ev=ve_ev):
                        for ob in range(8):
                            v.wait_ge(sem["PE"], pe_qk + ob + 1)
                            v.tensor_copy(Qf[:, ob, :],
                                          psE[ob % 2][:, 0:SLAB]).then_inc(sem["VE"], 1)

                    @block.sync
                    def _(sy, ve_ev=ve_ev):
                        sy.wait_ge(sem["LD"], pb.cnt["LD"])
                        sy.wait_ge(sem["VE"], ve_ev + 8)
                        for (d0, sc0) in ((0, 32), (32, 0), (64, 96), (96, 64)):
                            sy.dma_start(out=Qs[d0:d0 + 32, :, :],
                                         in_=Qf[sc0:sc0 + 32, :, :]).then_inc(sem["LD"], 16)
                    ld_swap = pb.bump("LD", 16 * 4)

                    ve_r0 = pb.reserve("VE", 2 * 8)
                    pea_total = pb.cnt["PEa"]

                    @block.vector
                    def _(v, ctab=ctab, stab=stab, which=which, ve_r0=ve_r0,
                          ld_swap=ld_swap, par=par, pea_total=pea_total):
                        v.wait_ge(sem["LD"], ld_swap)
                        if pea_total:
                            v.wait_ge(sem["PEa"], pea_total)
                        for ob in range(8):
                            dst = (qrot[:, ob, :] if which == "q"
                                   else krot[:, par, ob, :])
                            if ob >= 1:
                                v.wait_ge(sem["VE"], ve_r0 + 2 * ob)
                            v.tensor_mul(rt1[:, :], Qf[:, ob, :], ctab[:, :])
                            v.tensor_mul(rt2[:, :], Qs[:, ob, :], stab[:, :]).then_inc(sem["VE"], 1)
                            v.wait_ge(sem["VE"], ve_r0 + 2 * ob + 1)
                            v.tensor_add(dst, rt1[:, :], rt2[:, :]).then_inc(sem["VE"], 1)
                    ve_rope[which] = ve_r0 + 16

                pe_v0 = pb.reserve("PE", 2 * QTS)
                ve_v0 = pb.reserve("VE", 2 * QTS)
                peb_total = pb.cnt["PEb"]

                @block.tensor
                def _(tn, pe_v0=pe_v0, ve_v0=ve_v0):
                    tn.wait_ge(sem["VE"], ve_rope["k"])
                    for tc in range(QTS):
                        for hf in range(2):
                            if tc * 2 + hf >= 2:
                                tn.wait_ge(sem["VE"], ve_v0 + tc * 2 + hf - 1)
                            for cb in range(8):
                                mi = tn.matmul(psE[hf][:, :],
                                               XSb[:, cb, tc * 128:(tc + 1) * 128],
                                               WvTs[:, cb, hf * 512:(hf + 1) * 512],
                                               start=(cb == 0), stop=(cb == 7))
                            mi.then_inc(sem["PE"], 1)

                @block.vector
                def _(v, pe_v0=pe_v0, par=par, peb_total=peb_total):
                    if peb_total:
                        v.wait_ge(sem["PEb"], peb_total)
                    v.memset(vsb[:, par, :, :, 64:65], 1.0)
                    for tc in range(QTS):
                        for hf in range(2):
                            v.wait_ge(sem["PE"], pe_v0 + tc * 2 + hf + 1)
                            v.tensor_copy(
                                vsb[:, par, tc, hf * 8:(hf + 1) * 8, 0:64],
                                psE[hf][:, :].rearrange("p (h d) -> p h d", d=64)
                            ).then_inc(sem["VE"], 1)
                ve_v_done = ve_v0 + 2 * QTS

                for qt in range(QTS):
                    qg = jjj * QTS + qt
                    nv = min(3, qg + 1)
                    s0 = 3 - nv
                    pea0 = pb.reserve("PEa", NH + 8)
                    peb0 = pb.reserve("PEb", NH)
                    se0 = pb.reserve("SE", NH)
                    vea0 = pb.reserve("VEa", NH + 8)
                    veb0 = pb.reserve("VEb", 2 * NH)

                    def ktile(kti, h, qg=qg):
                        kg = qg - (2 - kti)
                        kpar = (kg // QTS) % 2
                        ktc = kg % QTS
                        ob, rb = h // 2, (h % 2) * 64
                        kr = krot[rb:rb + 64, kpar, ob, ktc * 128:(ktc + 1) * 128]
                        vv = vsb[:, kpar, ktc, h, 0:65]
                        return kr, vv

                    @block.tensor
                    def _(tn, qt=qt, qg=qg, nv=nv, s0=s0, pea0=pea0, peb0=peb0,
                          se0=se0, vea0=vea0, veb0=veb0, ktile=ktile,
                          pe_tr_prev=pe_tr_prev):
                        def emit_st(h):
                            if h < 2:
                                tn.wait_ge(sem["VE"], ve_rope["k"])
                                tn.wait_ge(sem["SE"], se0)
                            else:
                                tn.wait_ge(sem["SE"], se0 + h - 1)
                            ob, rb = h // 2, (h % 2) * 64
                            qr = qrot[rb:rb + 64, ob, qt * 128:(qt + 1) * 128]
                            for kti in range(s0, 3):
                                kr, _ = ktile(kti, h)
                                mi = tn.matmul(psS[h % 2][:, kti, :], kr, qr,
                                               start=True, stop=True)
                            mi.then_inc(sem["PEa"], 1)

                        def emit_pv(h):
                            tn.wait_ge(sem["VEa"], vea0 + h + 1)
                            if h < 2:
                                tn.wait_ge(sem["VEb"], veb0)
                                tn.wait_ge(sem["VE"], ve_v_done)
                            else:
                                tn.wait_ge(sem["VEb"], veb0 + 2 * (h - 2) + 2)
                            for kti in range(s0, 3):
                                _, vv = ktile(kti, h)
                                mi = tn.matmul(psO[h % 2][:, 0:65],
                                               PTmsk[:, h % 2, kti, :], vv,
                                               start=(kti == s0), stop=(kti == 2))
                            mi.then_inc(sem["PEb"], 1)

                        for h in range(NH):
                            emit_st(h)
                            if h >= 1:
                                emit_pv(h - 1)
                        emit_pv(NH - 1)
                        for ob in range(8):
                            tn.wait_ge(sem["VEb"], veb0 + 2 * (2 * ob + 1) + 2)
                            if ob >= 2:
                                tn.wait_ge(sem["VEa"], vea0 + NH + ob - 1)
                            tn.transpose(psTR[ob % 2][:, :],
                                         att[:, ob * 128:(ob + 1) * 128],
                                         identb[:, :]).then_inc(sem["PEa"], 1)

                    @block.scalar
                    def _(s_, nv=nv, s0=s0, pea0=pea0, se0=se0, vea0=vea0):
                        if s == 0 and jjj == 0 and qt == 0:
                            s_.wait_ge(sem["LD"], ld_consts)
                        for h in range(NH):
                            s_.wait_ge(sem["PEa"], pea0 + h + 1)
                            if h < 2:
                                s_.wait_ge(sem["VEa"], vea0)
                            else:
                                s_.wait_ge(sem["VEa"], vea0 + h - 1)
                            s_.activation(PTraw[:, h % 2, s0:3, :],
                                          psS[h % 2][:, s0:3, :],
                                          AF.Exp, bias=cshift[:, 0:1]).then_inc(sem["SE"], 1)

                    @block.vector
                    def _(v, nv=nv, s0=s0, qt=qt, se0=se0, peb0=peb0, vea0=vea0,
                          veb0=veb0, pea0=pea0, pe_tr_prev=pe_tr_prev):
                        for h in range(NH):
                            v.wait_ge(sem["SE"], se0 + h + 1)
                            if h < 2:
                                v.wait_ge(sem["PEb"], peb0)
                            else:
                                v.wait_ge(sem["PEb"], peb0 + h - 1)
                            v.scalar_tensor_tensor(
                                PTmsk[:, h % 2, s0:3, :], PTraw[:, h % 2, s0:3, :], 1.0,
                                maskT[:, s0:3, :], ALU.mult, ALU.mult).then_inc(sem["VEa"], 1)
                            v.wait_ge(sem["PEb"], peb0 + h + 1)
                            if pe_tr_prev is not None:
                                v.wait_ge(sem["PEa"], pe_tr_prev + NH + h // 2 + 1)
                            v.reciprocal(rinv[:, h % 2:h % 2 + 1],
                                         psO[h % 2][:, 64:65]).then_inc(sem["VEb"], 1)
                            v.wait_ge(sem["VEb"], veb0 + 2 * h + 1)
                            v.tensor_scalar_mul(att[:, h * 64:(h + 1) * 64],
                                                psO[h % 2][:, 0:64],
                                                rinv[:, h % 2:h % 2 + 1]).then_inc(sem["VEb"], 1)
                        for ob in range(8):
                            v.wait_ge(sem["PEa"], pea0 + NH + ob + 1)
                            v.tensor_copy(afm[:, ob, qt * 128:(qt + 1) * 128],
                                          psTR[ob % 2][:, :]).then_inc(sem["VEa"], 1)
                    pe_tr_prev = pea0

                ve_attn_done = pb.cnt["VEa"]

                pe_o0 = pb.reserve("PE", 8)
                ve_o0 = pb.reserve("VE", 8)

                @block.tensor
                def _(tn, pe_o0=pe_o0, ve_o0=ve_o0):
                    tn.wait_ge(sem["VEa"], ve_attn_done)
                    tn.wait_ge(sem["VE"], ve_v_done)
                    for ob in range(8):
                        if ob >= 2:
                            tn.wait_ge(sem["VE"], ve_o0 + ob - 1)
                        for cb in range(8):
                            mi = tn.matmul(psE[ob % 2][:, 0:SLAB],
                                           WoTs[:, cb, ob * 128:(ob + 1) * 128],
                                           afm[:, cb, :],
                                           start=(cb == 0), stop=(cb == 7))
                        mi.then_inc(sem["PE"], 1)

                @block.vector
                def _(v, pe_o0=pe_o0, st_x1=st_x1):
                    if st_x1:
                        v.wait_ge(sem["ST"], st_x1)
                    for ob in range(8):
                        v.wait_ge(sem["PE"], pe_o0 + ob + 1)
                        v.tensor_add(xo[:, ob, :], XS[:, ob, :],
                                     psE[ob % 2][:, 0:SLAB]).then_inc(sem["VE"], 1)
                ve_xo_resid = ve_o0 + 8

                ve_ln = emit_ln(pb, ln_ctx, xo, 8, SLAB, pre=[("VE", ve_xo_resid)])
                ve_ln_prev = ve_ln

                @block.sync
                def _(sy, t0=t0, s=s, ve_ln=ve_ln):
                    sy.wait_ge(sem["ST"], pb.cnt["ST"])
                    sy.wait_ge(sem["VE"], ve_ln)
                    sy.dma_start(out=hbm[xout_name][s, :, t0:t0 + SLAB]
                                 .rearrange("(c p) n -> p c n", p=128),
                                 in_=xo[:, :, :]).then_inc(sem["ST"], 16)
                st_x1 = pb.bump("ST", 16)
                ve_rope_k_prev = ve_rope["k"]
                pe_v_last = pe_v0 + 2 * QTS

        @block.sync
        def _(sy):
            sy.wait_ge(sem["ST"], pb.cnt["ST"])





B_DEBUG = {}


def emit_phaseB(pb, cfg, hbm, xin_name, xout_name):
    nc, block, sem = pb.nc, pb.block, pb.sem
    dbg_no_top2 = B_DEBUG.get("no_top2", False)
    dbg_nex = B_DEBUG.get("nex", None)
    S, T, TTB = cfg.S, cfg.T, cfg.TTB
    THN = TTB // 512
    TPS = T // TTB
    NT = S * TPS
    NEX = NE + 1
    start = pb.snap()

    with ExitStack() as st:
        x1b = sb(st, nc, "bx1b", [128, 8, TTB], BF16)
        acc = sb(st, nc, "bacc", [128, 8, TTB], F32)
        Wg = sb(st, nc, "bWg", [128, 8, I_FF], BF16)
        Wu = sb(st, nc, "bWu", [128, 8, I_FF], BF16)
        Wd = sb(st, nc, "bWd", [128, 16, D], BF16)
        hbf = sb(st, nc, "bhbf", [128, 16, TTB], BF16)
        sg = sb(st, nc, "bsg", [128, 2, 512], F32)
        uf = sb(st, nc, "buf", [128, 2, 512], F32)
        htmp = sb(st, nc, "bhtmp", [128, 2, 512], F32)
        bcw = sb(st, nc, "bbcw", [128, TTB], F32)
        probs = sb(st, nc, "bprobs", [8, TTB], F32)
        ptm = sb(st, nc, "bptm", [128, 8, 8], F32)
        wtm = sb(st, nc, "bwtm", [128, 8, 8], F32)
        w8 = sb(st, nc, "bw8", [8, TTB], F32)
        mrow = sb(st, nc, "bmrow", [128, 4], F32)
        g1 = sb(st, nc, "bg1", [128, 8], F32)
        pm = sb(st, nc, "bpm", [128, 8], F32)

        onec = sb(st, nc, "bonec", [128, 1], F32)
        oner = sb(st, nc, "boner", [1, 128], F32)
        epsln = sb(st, nc, "bepsln", [1, 1], F32)
        identf = sb(st, nc, "bidentf", [128, 128], F32)
        sel8s = sb(st, nc, "bsel8", [8, 8, 128], F32)
        rbias = sb(st, nc, "brbias", [8, 1], F32)
        WsR = sb(st, nc, "bWsR", [128, 8, 8], F32)
        psG = [ps(st, nc, f"bpsG{i}", [128, 512]) for i in range(2)]
        psU = [ps(st, nc, f"bpsU{i}", [128, 512]) for i in range(2)]
        psD = [ps(st, nc, f"bpsD{i}", [128, 512]) for i in range(2)]
        psR = [ps(st, nc, f"bpsR{i}", [8, 512]) for i in range(THN)]

        ln_ctx = dict(onec=onec, oner=oner, psA=psG[0], psB=psU[0],
                      t1=sg[:, 0, :], t2=sg[:, 1, :], r0=htmp[0:1, 0, :],
                      r1=htmp[0:1, 1, :], r2=uf[0:1, 0, :], eps_ap=epsln)

        def wsrc(e):
            if e == 0:
                return ("WsgT", "WsuT", "WsdT", None)
            return ("WgT", "WuT", "WdT", e - 1)

        def load_expert(g, e, parts=("g", "u", "d")):
            gn, un, dn, ei = wsrc(e)
            for pnm, dst in (("g", Wg), ("u", Wu), ("d", Wd)):
                if pnm not in parts:
                    continue
                nm = {"g": gn, "u": un, "d": dn}[pnm]
                src = hbm[nm] if ei is None else hbm[nm][ei]
                g.dma_start(out=dst[:], in_=r128(src)).then_inc(sem["LG"], 16)

        @block.sync
        def _(sy):
            pb.barrier(sy, start)
            sy.wait_ge(sem["LD"], pb.cnt["LD"])
            sy.dma_start(out=onec[:], in_=hbm["ones_col"]).then_inc(sem["LD"], 16)
            sy.dma_start(out=oner[:], in_=hbm["ones_row"]).then_inc(sem["LD"], 16)
            sy.dma_start(out=epsln[:], in_=hbm["eps_ln"]).then_inc(sem["LD"], 16)
            sy.dma_start(out=identf[:], in_=hbm["ident"]).then_inc(sem["LD"], 16)
            sy.dma_start(out=sel8s[:], in_=hbm["sel8"]).then_inc(sem["LD"], 16)
            sy.dma_start(out=rbias[:], in_=hbm["router_b"]).then_inc(sem["LD"], 16)
        ld_consts = pb.bump("LD", 16 * 6)

        @block.gpsimd
        def _(g):
            pb.barrier(g, start)
            g.wait_ge(sem["LG"], pb.cnt["LG"])
            g.dma_start(out=WsR[:], in_=r128(hbm["routerT"])).then_inc(sem["LG"], 16)
            load_expert(g, 0)
        lg_exp = pb.bump("LG", 16 * 4)
        lg_gu = {(0, 0): lg_exp}
        lg_d = {(0, 0): lg_exp}

        st_x2 = 0
        ve_ln_last = 0
        se_w8_prev = 0
        prev_ve_acc = 0
        psgu_free_ve = 0

        for tile in range(NT):
            s = tile // TPS
            t0 = (tile % TPS) * TTB

            @block.sync
            def _(sy, s=s, t0=t0, st_x2=st_x2, ve_ln_last=ve_ln_last):
                sy.wait_ge(sem["LD"], pb.cnt["LD"])
                if st_x2:
                    sy.wait_ge(sem["ST"], st_x2)
                if ve_ln_last:
                    sy.wait_ge(sem["VE"], ve_ln_last)
                sy.dma_start(out=acc[:], in_=hbm[xin_name][s, :, t0:t0 + TTB]
                             .rearrange("(c p) n -> p c n", p=128)).then_inc(sem["LD"], 16)
            ld_acc = pb.bump("LD", 16)

            @block.gpsimd
            def _(g, s=s, t0=t0):
                g.wait_ge(sem["LG"], pb.cnt["LG"])
                g.wait_ge(sem["PE"], pb.cnt["PE"])
                g.wait_ge(sem["PEb"], pb.cnt["PEb"])
                g.dma_start(out=x1b[:], in_=hbm[xin_name][s, :, t0:t0 + TTB]
                            .rearrange("(c p) n -> p c n", p=128)).then_inc(sem["LG"], 16)
            lg_x1b = pb.bump("LG", 16)

            pe_r0 = pb.reserve("PE", 1)

            @block.tensor
            def _(tn, lg_x1b=lg_x1b, se_w8_prev=se_w8_prev, ld_acc=ld_acc):
                tn.wait_ge(sem["LG"], lg_x1b)
                tn.wait_ge(sem["LD"], ld_acc)
                if se_w8_prev:
                    tn.wait_ge(sem["SE"], se_w8_prev)
                if tile == 0:
                    tn.wait_ge(sem["LG"], lg_exp)
                    tn.wait_ge(sem["LD"], ld_consts)
                for th in range(THN):
                    for cb in range(8):
                        mi = tn.matmul(psR[th][:, :],
                                       WsR[:, cb, :],
                                       acc[:, cb, th * 512:(th + 1) * 512],
                                       start=(cb == 0), stop=(cb == 7))
                mi.then_inc(sem["PE"], 1)

            @block.scalar
            def _(s_, pe_r0=pe_r0):
                s_.wait_ge(sem["PE"], pe_r0 + 1)
                for th in range(THN):
                    mi = s_.activation(probs[:, th * 512:(th + 1) * 512], psR[th][:, :],
                                       AF.Sigmoid, bias=rbias[:, 0:1])
                mi.then_inc(sem["SE"], 1)
            se_probs = pb.bump("SE")

            nch = 0 if dbg_no_top2 else TTB // 128
            chunk_vea = {}
            for c in range(nch):
                pe_t = pb.reserve("PEa", 1)

                @block.tensor
                def _(tn, c=c, pe_t=pe_t, prev_ve_acc=prev_ve_acc):
                    tn.wait_ge(sem["SE"], se_probs)
                    if c < 2:
                        if prev_ve_acc:
                            tn.wait_ge(sem["VE"], prev_ve_acc)
                    else:
                        tn.wait_ge(sem["VEa"], chunk_vea[c - 2] + 1)
                    tn.transpose(psD[c % 2][:, 0:8], probs[:, c * 128:(c + 1) * 128],
                                 identf[0:8, 0:8]).then_inc(sem["PEa"], 1)

                vea0 = pb.reserve("VEa", 8)
                chunk_vea[c] = vea0

                @block.vector
                def _(v, c=c, pe_t=pe_t, vea0=vea0):
                    v.wait_ge(sem["PEa"], pe_t + 1)
                    v.tensor_copy(ptm[:, c, :], psD[c % 2][:, 0:8]).then_inc(sem["VEa"], 1)
                    v.wait_ge(sem["VEa"], vea0 + 1)
                    v.reduce_max(mrow[:, 0:1], ptm[:, c, :], axis=AX.X).then_inc(sem["VEa"], 1)
                    v.wait_ge(sem["VEa"], vea0 + 2)
                    v.tensor_single_scalar(g1[:, :], ptm[:, c, :], mrow[:, 0:1],
                                           ALU.is_ge).then_inc(sem["VEa"], 1)
                    v.wait_ge(sem["VEa"], vea0 + 3)
                    v.scalar_tensor_tensor(pm[:, :], g1[:, :], -1e30, ptm[:, c, :],
                                           ALU.mult, ALU.add).then_inc(sem["VEa"], 1)
                    v.wait_ge(sem["VEa"], vea0 + 4)
                    v.reduce_max(mrow[:, 1:2], pm[:, :], axis=AX.X).then_inc(sem["VEa"], 1)
                    v.wait_ge(sem["VEa"], vea0 + 5)
                    v.tensor_add(mrow[:, 2:3], mrow[:, 0:1], mrow[:, 1:2])
                    v.tensor_single_scalar(g1[:, :], ptm[:, c, :], mrow[:, 1:2],
                                           ALU.is_ge).then_inc(sem["VEa"], 1)
                    v.wait_ge(sem["VEa"], vea0 + 6)
                    v.reciprocal(mrow[:, 3:4], mrow[:, 2:3])
                    v.tensor_mul(pm[:, :], ptm[:, c, :], g1[:, :]).then_inc(sem["VEa"], 1)
                    v.wait_ge(sem["VEa"], vea0 + 7)
                    v.tensor_single_scalar(wtm[:, c, :], pm[:, :], mrow[:, 3:4],
                                           ALU.mult).then_inc(sem["VEa"], 1)

            if not dbg_no_top2:
                pe_tb = pb.reserve("PEa", 1)

                @block.tensor
                def _(tn, pe_tb=pe_tb):
                    tn.wait_ge(sem["VEa"], pb.cnt["VEa"])
                    for c in range(nch):
                        mi = tn.transpose(psR[c // 4][:, (c % 4) * 128:(c % 4 + 1) * 128],
                                          wtm[:, c, :], identf[:, :])
                    mi.then_inc(sem["PEa"], 1)

                @block.scalar
                def _(s_, pe_tb=pe_tb):
                    s_.wait_ge(sem["PEa"], pe_tb + 1)
                    for th in range(THN):
                        mi = s_.copy(w8[:, th * 512:(th + 1) * 512], psR[th][:, :])
                    mi.then_inc(sem["SE"], 1)
                se_w8 = pb.bump("SE")
                se_w8_prev = se_w8
            else:
                @block.vector
                def _(v):
                    v.memset(w8[:, :], 0.1).then_inc(sem["VE"], 1)
                pb.bump("VE")
                se_w8 = 0

            NEXL = NEX if dbg_nex is None else dbg_nex
            for e in range(NEXL):
                if e > 0:
                    pe_bc = pb.reserve("PEb", THN)
                    ve_bc0 = pb.reserve("VE", THN)

                    @block.tensor
                    def _(tn, e=e, pe_bc=pe_bc, ve_bc0=ve_bc0,
                          prev_ve_acc=prev_ve_acc):
                        if se_w8:
                            tn.wait_ge(sem["SE"], se_w8)
                        if prev_ve_acc:
                            tn.wait_ge(sem["VE"], prev_ve_acc)
                        for th in range(THN):
                            if th > 0:
                                tn.wait_ge(sem["VE"], ve_bc0 + th)
                            mi = tn.matmul(psD[1][:, 0:512],
                                           sel8s[:, e - 1, :],
                                           w8[:, th * 512:(th + 1) * 512],
                                           start=True, stop=True)
                            mi.then_inc(sem["PEb"], 1)

                    @block.vector
                    def _(v, pe_bc=pe_bc, ve_bc0=ve_bc0):
                        for th in range(THN):
                            v.wait_ge(sem["PEb"], pe_bc + th + 1)
                            v.tensor_copy(bcw[:, th * 512:(th + 1) * 512],
                                          psD[1][:, :]).then_inc(sem["VE"], 1)
                    ve_bcw = ve_bc0 + THN
                else:
                    ve_bcw = 0

                NIDX = 16 * THN
                pe_g0 = pb.reserve("PE", NIDX)
                pe_u0 = pb.reserve("PE", NIDX)
                se_h0 = pb.reserve("SE", 2 * NIDX)
                ve_h0 = pb.reserve("VE", (2 if e > 0 else 1) * NIDX)
                nvh = 2 if e > 0 else 1

                @block.tensor
                def _(tn, e=e, pe_g0=pe_g0, pe_u0=pe_u0, ve_h0=ve_h0, nvh=nvh,
                      ve_bcw=ve_bcw, psgu_free_ve=psgu_free_ve):
                    tn.wait_ge(sem["LG"], lg_gu[(tile, e)])
                    if psgu_free_ve:
                        tn.wait_ge(sem["VE"], psgu_free_ve)
                    for pbi in range(16):
                        for th in range(THN):
                            idx = pbi * THN + th
                            if idx >= 2:
                                tn.wait_ge(sem["VE"], ve_h0 + nvh * (idx - 1))
                            for cb in range(8):
                                mg = tn.matmul(psG[idx % 2][:, :],
                                               Wg[:, cb, pbi * 128:(pbi + 1) * 128],
                                               x1b[:, cb, th * 512:(th + 1) * 512],
                                               start=(cb == 0), stop=(cb == 7))
                            mg.then_inc(sem["PE"], 1)
                            for cb in range(8):
                                mu = tn.matmul(psU[idx % 2][:, :],
                                               Wu[:, cb, pbi * 128:(pbi + 1) * 128],
                                               x1b[:, cb, th * 512:(th + 1) * 512],
                                               start=(cb == 0), stop=(cb == 7))
                            mu.then_inc(sem["PE"], 1)

                @block.scalar
                def _(s_, pe_g0=pe_g0, pe_u0=pe_u0, se_h0=se_h0, ve_h0=ve_h0,
                      nvh=nvh):
                    for idx in range(NIDX):
                        s_.wait_ge(sem["PE"], pe_g0 + 2 * idx + 1)
                        if idx >= 2:
                            s_.wait_ge(sem["VE"], ve_h0 + nvh * (idx - 1))
                        s_.activation(sg[:, idx % 2, :], psG[idx % 2][:, :],
                                      AF.Silu).then_inc(sem["SE"], 1)
                        s_.wait_ge(sem["PE"], pe_g0 + 2 * idx + 2)
                        s_.copy(uf[:, idx % 2, :], psU[idx % 2][:, :]).then_inc(sem["SE"], 1)

                @block.vector
                def _(v, e=e, se_h0=se_h0, ve_h0=ve_h0, nvh=nvh, ve_bcw=ve_bcw):
                    if e > 0:
                        v.wait_ge(sem["VE"], ve_bcw)
                    for pbi in range(16):
                        for th in range(THN):
                            idx = pbi * THN + th
                            v.wait_ge(sem["SE"], se_h0 + 2 * idx + 2)
                            if e == 0:
                                v.tensor_mul(hbf[:, pbi, th * 512:(th + 1) * 512],
                                             sg[:, idx % 2, :],
                                             uf[:, idx % 2, :]).then_inc(sem["VE"], 1)
                            else:
                                v.tensor_mul(htmp[:, idx % 2, :],
                                             sg[:, idx % 2, :],
                                             uf[:, idx % 2, :]).then_inc(sem["VE"], 1)
                                v.wait_ge(sem["VE"], ve_h0 + 2 * idx + 1)
                                v.tensor_mul(hbf[:, pbi, th * 512:(th + 1) * 512],
                                             htmp[:, idx % 2, :],
                                             bcw[:, th * 512:(th + 1) * 512]).then_inc(sem["VE"], 1)
                ve_h_done = ve_h0 + nvh * NIDX
                psgu_free_ve = ve_h_done

                nxt = e + 1
                ntile = tile
                if nxt == NEXL:
                    nxt = 0
                    ntile = tile + 1
                do_pf = ntile < NT

                if do_pf:
                    @block.gpsimd
                    def _(g, nxt=nxt, pe_g0=pe_g0):
                        g.wait_ge(sem["LG"], pb.cnt["LG"])
                        g.wait_ge(sem["PE"], pe_g0 + 2 * NIDX)
                        load_expert(g, nxt, parts=("g", "u"))
                    lg_gu[(ntile, nxt)] = pb.bump("LG", 16 * 2)

                pe_d0 = pb.reserve("PEb", 8 * THN)
                ve_d0 = pb.reserve("VE", 8 * THN)

                @block.tensor
                def _(tn, e=e, pe_d0=pe_d0, ve_d0=ve_d0, ve_h_done=ve_h_done):
                    tn.wait_ge(sem["VE"], ve_h_done)
                    tn.wait_ge(sem["LG"], lg_d[(tile, e)])
                    for ob in range(8):
                        for th in range(THN):
                            idx = ob * THN + th
                            if idx >= 2:
                                tn.wait_ge(sem["VE"], ve_d0 + idx - 1)
                            for cb in range(16):
                                md = tn.matmul(psD[idx % 2][:, :],
                                               Wd[:, cb, ob * 128:(ob + 1) * 128],
                                               hbf[:, cb, th * 512:(th + 1) * 512],
                                               start=(cb == 0), stop=(cb == 15))
                            md.then_inc(sem["PEb"], 1)

                @block.vector
                def _(v, pe_d0=pe_d0, ve_d0=ve_d0, ld_acc=ld_acc):
                    if e == 0:
                        v.wait_ge(sem["LD"], ld_acc)
                    for ob in range(8):
                        for th in range(THN):
                            idx = ob * THN + th
                            v.wait_ge(sem["PEb"], pe_d0 + idx + 1)
                            v.tensor_add(acc[:, ob, th * 512:(th + 1) * 512],
                                         acc[:, ob, th * 512:(th + 1) * 512],
                                         psD[idx % 2][:, :]).then_inc(sem["VE"], 1)
                ve_acc = ve_d0 + 8 * THN
                prev_ve_acc = ve_acc

                if do_pf:
                    @block.gpsimd
                    def _(g, nxt=nxt):
                        g.wait_ge(sem["LG"], pb.cnt["LG"])
                        g.wait_ge(sem["PEb"], pb.cnt["PEb"])
                        load_expert(g, nxt, parts=("d",))
                    lg_d[(ntile, nxt)] = pb.bump("LG", 16)

            ve_lns = []
            ve_pre = ve_acc
            for th in range(THN):
                xslice = acc[:, :, th * 512:(th + 1) * 512]
                ve_l = emit_ln(pb, ln_ctx, xslice, 8, 512,
                               pre=[("VE", ve_pre), ("PEb", pb.cnt["PEb"])])
                ve_lns.append(ve_l)
                ve_pre = ve_l
            ve_ln_last = ve_lns[-1]
            psgu_free_ve = ve_ln_last

            @block.sync
            def _(sy, s=s, t0=t0, ve_lns=ve_lns):
                sy.wait_ge(sem["ST"], pb.cnt["ST"])
                for v in ve_lns:
                    sy.wait_ge(sem["VE"], v)
                sy.dma_start(out=hbm[xout_name][s, :, t0:t0 + TTB]
                             .rearrange("(c p) n -> p c n", p=128),
                             in_=acc[:, :, :]).then_inc(sem["ST"], 16)
            st_x2 = pb.bump("ST", 16)

        @block.sync
        def _(sy):
            sy.wait_ge(sem["ST"], pb.cnt["ST"])


PHASE_EMIT["B"] = emit_phaseB





def emit_phaseC(pb, cfg, hbm, xin_name, xout_name):
    nc, block, sem = pb.nc, pb.block, pb.sem
    S, T, TC = cfg.S, cfg.T, cfg.TC
    NCH = T // TC
    start = pb.snap()

    with ExitStack() as st:
        XC = sb(st, nc, "cXC", [128, 8, TC], F32)
        hb = sb(st, nc, "chb", [128, 8, TC], BF16)
        WipU = sb(st, nc, "cWipU", [128, 8, DI], BF16)
        Wsh = sb(st, nc, "cWsh", [128, 16384], BF16)
        u_sb = sb(st, nc, "cu", [128, 16, TC + 3], F32)
        utail = sb(st, nc, "cutail", [128, 16, 3], F32)
        uc = sb(st, nc, "cuc", [128, 16, TC], F32)
        dt = sb(st, nc, "cdt", [128, 16, TC], F32)
        zb = sb(st, nc, "cz", [128, 16, TC], F32)
        yb = sb(st, nc, "cy", [128, 16, TC], F32)
        dBu = sb(st, nc, "cdBu", [128, 16, TC], F32)
        ygb = sb(st, nc, "cyg", [128, 16, TC], BF16)
        bcB = sb(st, nc, "cbcB", [128, 1, TC], F32)
        bcC = sb(st, nc, "cbcC", [128, 1, TC], F32)
        carry = sb(st, nc, "ccarry", [128, 16, NSTATE], F32)
        xdbl = sb(st, nc, "cxdbl", [128, TC], F32)
        dtin = sb(st, nc, "cdtin", [64, TC], BF16)
        spx = sb(st, nc, "cspx", [128, 2, TC], F32)
        dA2 = spx
        spa = ct1 = sb(st, nc, "cct1", [128, TC], F32)
        ct2 = sb(st, nc, "cct2", [128, TC], F32)
        spr = ct2
        WxpS = sb(st, nc, "cWxp", [128, 16, 96], BF16)
        WdtS = sb(st, nc, "cWdt", [64, 16, 128], BF16)
        ecol = sb(st, nc, "cecol", [32, 32], F32)
        bc_sb = sb(st, nc, "cbcsb", [32, TC], F32)

        Acol = sb(st, nc, "cAcol", [128, 16, NSTATE], F32)
        AlogS = Acol
        normw = sb(st, nc, "cnormw", [128, 8], F32)
        convw = sb(st, nc, "cconvw", [128, 16, KCONV], F32)
        convb = sb(st, nc, "cconvb", [128, 16], F32)
        dtb = sb(st, nc, "cdtb", [128, 16], F32)
        Dcol = sb(st, nc, "cDcol", [128, 16], F32)
        t1, t2 = ct1, ct2
        rowbuf = sb(st, nc, "crowbuf", [1, 7, TC], F32)
        rows = rowbuf[:, 0:3, :]
        rowB = rowbuf[:, 3:5, :]
        rowC = rowbuf[:, 5:7, :]
        onec = sb(st, nc, "conec", [128, 1], F32)
        oner = sb(st, nc, "coner", [1, 128], F32)
        epsln = sb(st, nc, "cepsln", [1, 1], F32)
        epsrms = sb(st, nc, "cepsrms", [1, 1], F32)
        psI = [ps(st, nc, f"cpsI{i}", [128, 512]) for i in range(2)]
        psX = ps(st, nc, "cpsX", [128, 512])
        psB2 = [ps(st, nc, f"cpsB2{i}", [128, 512]) for i in range(2)]
        psC2 = [ps(st, nc, f"cpsC2{i}", [128, 512]) for i in range(2)]

        ln_ctx = dict(onec=onec, oner=oner, psA=psI[0], psB=psI[1],
                      t1=t1, t2=t2, r0=rows[:, 0, :], r1=rows[:, 1, :],
                      r2=rows[:, 2, :], eps_ap=epsln)

        @block.sync
        def _(sy):
            pb.barrier(sy, start)
            sy.wait_ge(sem["LD"], pb.cnt["LD"])
            for dst, nm in ((onec, "ones_col"), (oner, "ones_row"),
                            (epsln, "eps_ln"), (epsrms, "eps_rms"),
                            (ecol, "esel_cols"), (AlogS, "Alog_fm"),
                            (normw, "normw_fm"), (convw, "convw_fm"),
                            (convb, "convb_fm"), (dtb, "dtb_fm"),
                            (Dcol, "D_fm")):
                sy.dma_start(out=dst[:], in_=hbm[nm]).then_inc(sem["LD"], 16)
        ld_consts = pb.bump("LD", 16 * 11)

        @block.gpsimd
        def _(g):
            pb.barrier(g, start)
            g.wait_ge(sem["LG"], pb.cnt["LG"])
            g.dma_start(out=WipU[:], in_=hbm["WipT"][:, 0:DI]
                        .rearrange("(c p) n -> p c n", p=128)).then_inc(sem["LG"], 16)
            g.dma_start(out=WxpS[:], in_=hbm["WxpT"]
                        .rearrange("(c p) n -> p c n", p=128)).then_inc(sem["LG"], 16)
            g.dma_start(out=WdtS[:], in_=hbm["WdtT"]
                        .rearrange("c (b n) -> c b n", n=128)).then_inc(sem["LG"], 16)
        lg_init = pb.bump("LG", 16 * 3)

        se_A0 = pb.reserve("SE", 1)
        ve_A0 = pb.reserve("VE", 1)

        @block.scalar
        def _(s_):
            s_.wait_ge(sem["LD"], ld_consts)
            s_.activation(Acol[:, :, :], Acol[:, :, :], AF.Exp).then_inc(sem["SE"], 1)

        @block.vector
        def _(v):
            v.wait_ge(sem["SE"], se_A0 + 1)
            v.tensor_scalar_mul(Acol[:, :, :], Acol[:, :, :], -1.0).then_inc(sem["VE"], 1)
        ve_Acol = ve_A0 + 1

        st_x3 = 0
        ve_ln_prev = 0
        pe_op_prev = 0
        se_xdbl_prev = 0
        ve_utail_prev = 0
        prev_scan_done = 0

        for si in range(S):
            for ci in range(NCH):
                t0 = ci * TC
                first = (si == 0 and ci == 0)

                @block.sync
                def _(sy, si=si, t0=t0, st_x3=st_x3):
                    sy.wait_ge(sem["LD"], pb.cnt["LD"])
                    if st_x3:
                        sy.wait_ge(sem["ST"], st_x3)
                    if ve_ln_prev:
                        sy.wait_ge(sem["VE"], ve_ln_prev)
                    sy.dma_start(out=XC[:], in_=hbm[xin_name][si, :, t0:t0 + TC]
                                 .rearrange("(c p) n -> p c n", p=128)).then_inc(sem["LD"], 16)
                ld_xc = pb.bump("LD", 16)

                @block.gpsimd
                def _(g, pe_op_prev=pe_op_prev):
                    g.wait_ge(sem["LG"], pb.cnt["LG"])
                    if pe_op_prev:
                        g.wait_ge(sem["PE"], pe_op_prev)
                    g.dma_start(out=Wsh[:, :].rearrange("p (c n) -> p c n", c=8),
                                in_=hbm["WipT"][:, DI:2 * DI]
                                .rearrange("(c p) n -> p c n", p=128)).then_inc(sem["LG"], 16)
                lg_z = pb.bump("LG", 16)

                se_sq0 = pb.reserve("SE", 8)
                pe_sq0 = pb.reserve("PE", 8)

                @block.scalar
                def _(s_, ld_xc=ld_xc, se_sq0=se_sq0, pe_sq0=pe_sq0, first=first,
                      pe_tot=None):
                    s_.wait_ge(sem["LD"], ld_xc)
                    if first:
                        s_.wait_ge(sem["LD"], ld_consts)
                    if ve_ln_prev:
                        s_.wait_ge(sem["VE"], ve_ln_prev)
                    s_.wait_ge(sem["PE"], pe_sq0)
                    for cb in range(8):
                        if cb >= 2:
                            s_.wait_ge(sem["PE"], pe_sq0 + cb - 1)
                        s_.activation((t1 if cb % 2 == 0 else t2)[:, 0:TC],
                                      XC[:, cb, :], AF.Square).then_inc(sem["SE"], 1)

                @block.tensor
                def _(tn, se_sq0=se_sq0):
                    if ve_ln_prev:
                        tn.wait_ge(sem["VE"], ve_ln_prev)
                    for cb in range(8):
                        tn.wait_ge(sem["SE"], se_sq0 + cb + 1)
                        tn.matmul(psI[0][0:1, 0:TC], onec[:, 0:1],
                                  (t1 if cb % 2 == 0 else t2)[:, 0:TC],
                                  start=(cb == 0), stop=(cb == 7)).then_inc(sem["PE"], 1)

                se_r0 = pb.reserve("SE", 2)

                @block.scalar
                def _(s_, pe_sq0=pe_sq0, se_r0=se_r0):
                    s_.wait_ge(sem["PE"], pe_sq0 + 8)
                    s_.activation(rows[:, 0, 0:TC], psI[0][0:1, 0:TC], AF.Ln,
                                  scale=1.0 / D, bias=epsrms[:, :]).then_inc(sem["SE"], 1)
                    s_.wait_ge(sem["SE"], se_r0 + 1)
                    s_.activation(rows[:, 1, 0:TC], rows[:, 0, 0:TC], AF.Exp,
                                  scale=-0.5).then_inc(sem["SE"], 1)

                pe_bc0 = pb.reserve("PE", 1)

                @block.tensor
                def _(tn, se_r0=se_r0):
                    tn.wait_ge(sem["SE"], se_r0 + 2)
                    tn.matmul(psB2[0][:, 0:TC], oner[0:1, :], rows[:, 1, 0:TC],
                              start=True, stop=True).then_inc(sem["PE"], 1)

                ve_h0 = pb.reserve("VE", 8)

                @block.vector
                def _(v, pe_bc0=pe_bc0, first=first):
                    if first:
                        v.wait_ge(sem["VE"], ve_Acol)
                    v.wait_ge(sem["PE"], pe_bc0 + 1)
                    for cb in range(8):
                        v.scalar_tensor_tensor(hb[:, cb, :], XC[:, cb, :],
                                               normw[:, cb:cb + 1], psB2[0][:, 0:TC],
                                               ALU.mult, ALU.mult).then_inc(sem["VE"], 1)
                ve_hb = ve_h0 + 8

                if ci == 0:
                    ve_c0 = pb.reserve("VE", 1)

                    @block.vector
                    def _(v):
                        v.memset(carry[:, :, :], 0.0)
                        v.memset(utail[:, :, :], 0.0).then_inc(sem["VE"], 1)
                    ve_utail_prev = ve_c0 + 1

                pe_u0 = pb.reserve("PE", 16)
                se_u0 = pb.reserve("SE", 16)

                @block.tensor
                def _(tn, pe_u0=pe_u0, se_u0=se_u0, first=first):
                    tn.wait_ge(sem["VE"], ve_hb)
                    if first:
                        tn.wait_ge(sem["LG"], lg_init)
                    for pbi in range(16):
                        if pbi >= 2:
                            tn.wait_ge(sem["SE"], se_u0 + pbi - 1)
                        for cb in range(8):
                            mi = tn.matmul(psI[pbi % 2][:, 0:TC],
                                           WipU[:, cb, pbi * 128:(pbi + 1) * 128],
                                           hb[:, cb, :],
                                           start=(cb == 0), stop=(cb == 7))
                        mi.then_inc(sem["PE"], 1)

                @block.scalar
                def _(s_, pe_u0=pe_u0, ve_utail_prev=ve_utail_prev,
                      vlp=ve_ln_prev, gp_now=pb.cnt["GP"]):
                    s_.wait_ge(sem["VE"], ve_utail_prev)
                    if vlp:
                        s_.wait_ge(sem["VE"], vlp)
                    if gp_now:
                        s_.wait_ge(sem["GP"], gp_now)
                    for pbi in range(16):
                        s_.wait_ge(sem["PE"], pe_u0 + pbi + 1)
                        s_.copy(u_sb[:, pbi, 3:3 + TC],
                                psI[pbi % 2][:, 0:TC]).then_inc(sem["SE"], 1)

                pe_z0 = pb.reserve("PE", 16)
                se_z0 = pb.reserve("SE", 16)

                @block.tensor
                def _(tn, pe_z0=pe_z0, se_z0=se_z0, se_u0=se_u0, lg_z=lg_z):
                    tn.wait_ge(sem["LG"], lg_z)
                    tn.wait_ge(sem["SE"], se_u0 + 16)
                    for pbi in range(16):
                        if pbi >= 2:
                            tn.wait_ge(sem["SE"], se_z0 + pbi - 1)
                        for cb in range(8):
                            mi = tn.matmul(psI[pbi % 2][:, 0:TC],
                                           Wsh[:, cb * 2048 + pbi * 128:
                                               cb * 2048 + (pbi + 1) * 128],
                                           hb[:, cb, :],
                                           start=(cb == 0), stop=(cb == 7))
                        mi.then_inc(sem["PE"], 1)
                pe_z_done = pe_z0 + 16

                @block.scalar
                def _(s_, pe_z0=pe_z0, vlp=ve_ln_prev):
                    if vlp:
                        s_.wait_ge(sem["VE"], vlp)
                    for pbi in range(16):
                        s_.wait_ge(sem["PE"], pe_z0 + pbi + 1)
                        s_.copy(zb[:, pbi, :],
                                psI[pbi % 2][:, 0:TC]).then_inc(sem["SE"], 1)

                @block.gpsimd
                def _(g, pe_z_done=pe_z_done):
                    g.wait_ge(sem["LG"], pb.cnt["LG"])
                    g.wait_ge(sem["PE"], pe_z_done)
                    g.dma_start(out=Wsh[:, :].rearrange("p (c n) -> p c n", c=16),
                                in_=hbm["WopT"]
                                .rearrange("(c p) n -> p c n", p=128)).then_inc(sem["LG"], 16)
                lg_op = pb.bump("LG", 16)

                ve_tl = pb.reserve("VE", 1)

                @block.vector
                def _(v, se_u0=se_u0):
                    v.wait_ge(sem["SE"], se_u0 + 16)
                    v.tensor_copy(u_sb[:, :, 0:3], utail[:, :, :]).then_inc(sem["VE"], 1)

                ve_cv0 = pb.reserve("VE", 4 * 16)
                se_cv0 = pb.reserve("SE", 16)

                @block.vector
                def _(v, ve_cv0=ve_cv0, se_cv0=se_cv0, ve_tl=ve_tl, pe_sq0=pe_sq0):
                    v.wait_ge(sem["VE"], ve_tl + 1)
                    v.wait_ge(sem["PE"], pe_sq0 + 8)
                    for pbi in range(16):
                        if pbi >= 1:
                            v.wait_ge(sem["VE"], ve_cv0 + 4 * pbi)
                            v.wait_ge(sem["SE"], se_cv0 + pbi)
                        v.tensor_scalar(ct1[:, :], u_sb[:, pbi, 0:TC],
                                        convw[:, pbi, 0:1], convb[:, pbi:pbi + 1],
                                        ALU.mult, ALU.add).then_inc(sem["VE"], 1)
                        v.wait_ge(sem["VE"], ve_cv0 + 4 * pbi + 1)
                        v.scalar_tensor_tensor(ct2[:, :], u_sb[:, pbi, 1:1 + TC],
                                               convw[:, pbi, 1:2], ct1[:, :],
                                               ALU.mult, ALU.add).then_inc(sem["VE"], 1)
                        v.wait_ge(sem["VE"], ve_cv0 + 4 * pbi + 2)
                        v.scalar_tensor_tensor(ct1[:, :], u_sb[:, pbi, 2:2 + TC],
                                               convw[:, pbi, 2:3], ct2[:, :],
                                               ALU.mult, ALU.add).then_inc(sem["VE"], 1)
                        v.wait_ge(sem["VE"], ve_cv0 + 4 * pbi + 3)
                        v.scalar_tensor_tensor(ct2[:, :], u_sb[:, pbi, 3:3 + TC],
                                               convw[:, pbi, 3:4], ct1[:, :],
                                               ALU.mult, ALU.add).then_inc(sem["VE"], 1)

                @block.scalar
                def _(s_, ve_cv0=ve_cv0):
                    for pbi in range(16):
                        s_.wait_ge(sem["VE"], ve_cv0 + 4 * pbi + 4)
                        s_.activation(uc[:, pbi, :], ct2[:, :],
                                      AF.Silu).then_inc(sem["SE"], 1)
                se_uc = se_cv0 + 16

                ve_ut = pb.reserve("VE", 1)

                @block.vector
                def _(v):
                    v.tensor_copy(utail[:, :, :],
                                  u_sb[:, :, TC:TC + 3]).then_inc(sem["VE"], 1)
                ve_utail_prev = ve_ut + 1

                ve_ucb = pb.reserve("VE", 1)

                @block.vector
                def _(v, se_cv0=se_cv0, pe_op_prev=pe_op_prev):
                    v.wait_ge(sem["SE"], se_cv0 + 16)
                    if pe_op_prev:
                        v.wait_ge(sem["PE"], pe_op_prev)
                    v.tensor_copy(ygb[:, :, :], uc[:, :, :]).then_inc(sem["VE"], 1)

                pe_xd = pb.reserve("PE", 1)

                @block.tensor
                def _(tn, ve_ucb=ve_ucb, se_xdbl_prev=se_xdbl_prev):
                    tn.wait_ge(sem["VE"], ve_ucb + 1)
                    tn.wait_ge(sem["PEb"], pb.cnt["PEb"])
                    if se_xdbl_prev:
                        tn.wait_ge(sem["SE"], se_xdbl_prev)
                    for cb in range(16):
                        tn.matmul(psX[0:64, 0:TC], WxpS[:, cb, 0:64],
                                  ygb[:, cb, :], start=(cb == 0), stop=(cb == 15))
                    for cb in range(16):
                        mi = tn.matmul(psX[0:32, 256:256 + TC], WxpS[:, cb, 64:96],
                                       ygb[:, cb, :], start=(cb == 0), stop=(cb == 15))
                    mi.then_inc(sem["PE"], 1)

                se_xd = pb.reserve("SE", 1)

                @block.scalar
                def _(s_, pe_xd=pe_xd):
                    s_.wait_ge(sem["PE"], pe_xd + 1)
                    s_.copy(xdbl[0:64, :], psX[0:64, 0:TC])
                    s_.copy(bc_sb[:, :], psX[0:32, 256:256 + TC]).then_inc(sem["SE"], 1)
                se_xdbl_prev = se_xd + 1

                ve_dtin = pb.reserve("VE", 1)

                @block.vector
                def _(v, se_xd=se_xd):
                    v.wait_ge(sem["SE"], se_xd + 1)
                    v.tensor_copy(dtin[:, :], xdbl[0:64, :]).then_inc(sem["VE"], 1)

                pe_dt0 = pb.reserve("PE", 16)
                se_sp0 = pb.reserve("SE", 4 * 16)
                ve_sp0 = pb.reserve("VE", 2 * 16)

                @block.tensor
                def _(tn, pe_dt0=pe_dt0, se_sp0=se_sp0, se_z0=se_z0, ve_dtin=ve_dtin):
                    tn.wait_ge(sem["VE"], ve_dtin + 1)
                    tn.wait_ge(sem["SE"], se_z0 + 16)
                    for pbi in range(16):
                        if pbi >= 2:
                            tn.wait_ge(sem["SE"], se_sp0 + 4 * (pbi - 2) + 1)
                        tn.matmul(psI[pbi % 2][:, 0:TC], WdtS[:, pbi, :],
                                  dtin[:, :], start=True, stop=True).then_inc(sem["PE"], 1)

                @block.scalar
                def _(s_, pe_dt0=pe_dt0, se_sp0=se_sp0, ve_sp0=ve_sp0):
                    for pbi in range(16):
                        s_.wait_ge(sem["PE"], pe_dt0 + pbi + 1)
                        if pbi >= 2:
                            s_.wait_ge(sem["VE"], ve_sp0 + 2 * (pbi - 2) + 2)
                        s_.activation(spx[:, pbi % 2, :], psI[pbi % 2][:, 0:TC],
                                      AF.Identity, bias=dtb[:, pbi:pbi + 1]).then_inc(sem["SE"], 1)
                        s_.wait_ge(sem["SE"], se_sp0 + 4 * pbi + 1)
                        if pbi >= 1:
                            s_.wait_ge(sem["VE"], ve_sp0 + 2 * pbi)
                        s_.activation(spa[:, :], spx[:, pbi % 2, :],
                                      AF.Abs).then_inc(sem["SE"], 1)
                        s_.wait_ge(sem["SE"], se_sp0 + 4 * pbi + 2)
                        s_.activation(spa[:, :], spa[:, :], AF.Exp,
                                      scale=-1.0).then_inc(sem["SE"], 1)
                        s_.wait_ge(sem["SE"], se_sp0 + 4 * pbi + 3)
                        s_.activation(spa[:, :], spa[:, :], AF.Ln,
                                      bias=onec[:, 0:1]).then_inc(sem["SE"], 1)

                @block.vector
                def _(v, se_sp0=se_sp0, ve_sp0=ve_sp0):
                    for pbi in range(16):
                        v.wait_ge(sem["SE"], se_sp0 + 4 * pbi + 4)
                        v.tensor_scalar_max(spr[:, :], spx[:, pbi % 2, :],
                                            0.0).then_inc(sem["VE"], 1)
                        v.wait_ge(sem["VE"], ve_sp0 + 2 * pbi + 1)
                        v.tensor_add(dt[:, pbi, :], spr[:, :],
                                     spa[:, :]).then_inc(sem["VE"], 1)
                ve_dt_done = ve_sp0 + 32

                ve_dtu = pb.reserve("VE", 1)

                @block.vector
                def _(v, ve_dt_done=ve_dt_done):
                    v.wait_ge(sem["VE"], ve_dt_done)
                    v.tensor_mul(u_sb[:, :, 0:TC], dt[:, :, :],
                                 uc[:, :, :]).then_inc(sem["VE"], 1)
                ve_dtu_done = ve_dtu + 1

                scan_base = {}
                bc_base = {}
                row_se = {}
                gp_dbu = {}
                prev_gp_dict = None
                prev_gp = 0
                for n in range(NSTATE):
                    pe_b1 = pb.reserve("PEb", 1)

                    @block.tensor
                    def _(tn, n=n, se_xd=se_xd):
                        tn.wait_ge(sem["SE"], se_xd + 1)
                        if n < 2:
                            tn.wait_ge(sem["VE"], ve_hb)
                            if prev_scan_done:
                                tn.wait_ge(sem["VE"], prev_scan_done)
                        else:
                            tn.wait_ge(sem["VEa"], bc_base[n - 2] + 2)
                            tn.wait_ge(sem["SE"], row_se[n - 2] + 1)
                        tn.matmul(psB2[n % 2][0:1, 256:256 + TC], ecol[:, n:n + 1],
                                  bc_sb[:, :], start=True, stop=True)
                        tn.matmul(psC2[n % 2][0:1, 256:256 + TC], ecol[:, 16 + n:17 + n],
                                  bc_sb[:, :], start=True,
                                  stop=True).then_inc(sem["PEb"], 1)

                    se_row = pb.reserve("SE", 1)
                    row_se[n] = se_row

                    @block.scalar
                    def _(s_, n=n, pe_b1=pe_b1, se_row=se_row):
                        s_.wait_ge(sem["PEb"], pe_b1 + 1)
                        if n >= 2:
                            s_.wait_ge(sem["PEb"], pb.cnt["PEb"] - 1)
                        s_.copy(rowB[:, n % 2, :], psB2[n % 2][0:1, 256:256 + TC])
                        s_.copy(rowC[:, n % 2, :],
                                psC2[n % 2][0:1, 256:256 + TC]).then_inc(sem["SE"], 1)

                    pe_b2 = pb.reserve("PEb", 1)

                    @block.tensor
                    def _(tn, n=n, se_row=se_row):
                        tn.wait_ge(sem["SE"], se_row + 1)
                        tn.matmul(psB2[n % 2][:, 0:TC], oner[0:1, :],
                                  rowB[:, n % 2, :], start=True, stop=True)
                        tn.matmul(psC2[n % 2][:, 0:TC], oner[0:1, :],
                                  rowC[:, n % 2, :], start=True,
                                  stop=True).then_inc(sem["PEb"], 1)

                    vea0 = pb.reserve("VEa", 2)
                    bc_base[n] = vea0

                    @block.vector
                    def _(v, n=n, pe_b2=pe_b2, vea0=vea0,
                          gp_dbu_prev=(gp_dbu.get(n - 1) if n >= 1 else None)):
                        v.wait_ge(sem["PEb"], pe_b2 + 1)
                        if n >= 1:
                            v.wait_ge(sem["VE"], prev_scan_done)
                            v.wait_ge(sem["GP"], gp_dbu_prev)
                        v.tensor_copy(bcB[:, 0, :], psB2[n % 2][:, 0:TC]).then_inc(sem["VEa"], 1)
                        v.wait_ge(sem["VEa"], vea0 + 1)
                        v.tensor_copy(bcC[:, 0, :], psC2[n % 2][:, 0:TC]).then_inc(sem["VEa"], 1)

                    HB = NSTATE // 2
                    se_sc0 = pb.reserve("SE", NSTATE)
                    ngp = 2 if n == 0 else 4
                    gp0 = pb.reserve("GP", ngp)
                    gp_dbu[n] = gp0 + ngp
                    nve = NSTATE + 4
                    ve_s0 = pb.reserve("VE", nve)
                    scan_base[n] = ve_s0
                    prev_sd = prev_scan_done
                    ve_ymul = {0: ve_s0 + HB + 2, 1: ve_s0 + 2 * HB + 4}
                    gp_dbu_h = {0: gp0 + 1, 1: gp0 + 2}
                    gp_yadd_h = {0: gp0 + 3, 1: gp0 + 4}

                    @block.scalar
                    def _(s_, n=n, se_sc0=se_sc0, ve_s0=ve_s0):
                        if n == 0:
                            s_.wait_ge(sem["VE"], ve_dt_done)
                        for pbi in range(NSTATE):
                            if pbi < 2:
                                if n > 0:
                                    s_.wait_ge(sem["VE"], prev_sd)
                            else:
                                off = pbi - 2
                                v = ve_s0 + off + 1 + (2 if off >= HB else 0)
                                s_.wait_ge(sem["VE"], v)
                            s_.activation(dA2[:, pbi % 2, :], dt[:, pbi, :], AF.Exp,
                                          scale=Acol[:, pbi, n:n + 1]).then_inc(sem["SE"], 1)

                    @block.gpsimd
                    def _(g, n=n, vea0=vea0, gp0=gp0, prev_sd=prev_sd,
                          pgd=prev_gp_dict):
                        g.wait_ge(sem["VEa"], vea0 + 2)
                        g.wait_ge(sem["GP"], gp0)
                        for h in range(2):
                            sl = slice(h * HB, (h + 1) * HB)
                            if n == 0:
                                if h == 0:
                                    g.wait_ge(sem["VE"], ve_dtu_done)
                            else:
                                g.wait_ge(sem["VE"], pgd["ymul"][h])
                            g.tensor_tensor(dBu[:, sl, :], u_sb[:, sl, 0:TC],
                                            bcB[:, 0:1, :].broadcast_to([128, HB, TC]),
                                            ALU.mult).then_inc(sem["GP"], 1)

                    @block.vector
                    def _(v, n=n, se_sc0=se_sc0, ve_s0=ve_s0, gp0=gp0):
                        bcCb = bcC[:, 0:1, :].broadcast_to([128, HB, TC])
                        for h in range(2):
                            sl = slice(h * HB, (h + 1) * HB)
                            v.wait_ge(sem["GP"], gp_dbu_h[h])
                            for j in range(HB):
                                pbi = h * HB + j
                                v.wait_ge(sem["SE"], se_sc0 + pbi + 1)
                                v.tensor_tensor_scan(dBu[:, pbi, :], dA2[:, pbi % 2, :],
                                                     dBu[:, pbi, :],
                                                     carry[:, pbi, n:n + 1],
                                                     ALU.mult, ALU.add).then_inc(sem["VE"], 1)
                            base_h = ve_s0 + h * (HB + 2)
                            v.wait_ge(sem["VE"], base_h + HB)
                            v.tensor_copy(carry[:, sl, n:n + 1],
                                          dBu[:, sl, TC - 1:TC]).then_inc(sem["VE"], 1)
                            v.wait_ge(sem["VE"], base_h + HB + 1)
                            if n == 0:
                                v.tensor_mul(yb[:, sl, :], dBu[:, sl, :],
                                             bcCb).then_inc(sem["VE"], 1)
                            else:
                                v.tensor_mul(dBu[:, sl, :], dBu[:, sl, :],
                                             bcCb).then_inc(sem["VE"], 1)

                    if n > 0:
                        @block.gpsimd
                        def _(g, n=n, ve_s0=ve_s0):
                            for h in range(2):
                                sl = slice(h * HB, (h + 1) * HB)
                                g.wait_ge(sem["VE"], ve_ymul[h])
                                g.tensor_tensor(yb[:, sl, :], yb[:, sl, :],
                                                dBu[:, sl, :],
                                                ALU.add).then_inc(sem["GP"], 1)
                    prev_gp_dict = {"ymul": ve_ymul, "yadd": gp_yadd_h}
                    prev_scan_done = ve_s0 + nve

                se_sz = pb.reserve("SE", 1)

                @block.scalar
                def _(s_, se_z0=se_z0):
                    s_.wait_ge(sem["SE"], se_z0 + 16)
                    s_.wait_ge(sem["VE"], prev_scan_done)
                    s_.activation(zb[:, :, :], zb[:, :, :], AF.Silu).then_inc(sem["SE"], 1)

                ve_g0 = pb.reserve("VE", 17)

                @block.vector
                def _(v, se_sz=se_sz, ve_g0=ve_g0, pe_xd=pe_xd,
                      prev_sd=prev_scan_done, gp_now=pb.cnt["GP"]):
                    v.wait_ge(sem["SE"], se_sz + 1)
                    v.wait_ge(sem["VE"], prev_sd)
                    v.wait_ge(sem["GP"], gp_now)
                    v.wait_ge(sem["PE"], pe_xd + 1)
                    for pbi in range(16):
                        v.scalar_tensor_tensor(yb[:, pbi, :], uc[:, pbi, :],
                                               Dcol[:, pbi:pbi + 1], yb[:, pbi, :],
                                               ALU.mult, ALU.add).then_inc(sem["VE"], 1)
                    v.wait_ge(sem["VE"], ve_g0 + 16)
                    v.tensor_mul(ygb[:, :, :], yb[:, :, :],
                                 zb[:, :, :]).then_inc(sem["VE"], 1)
                ve_yg = ve_g0 + 17

                pe_o0 = pb.reserve("PE", 8)
                ve_o0 = pb.reserve("VE", 8)

                @block.tensor
                def _(tn, pe_o0=pe_o0, ve_o0=ve_o0, lg_op=lg_op, ve_yg=ve_yg,
                      se_sp0=se_sp0):
                    tn.wait_ge(sem["LG"], lg_op)
                    tn.wait_ge(sem["VE"], ve_yg)
                    tn.wait_ge(sem["SE"], se_sp0 + 64)
                    for ob in range(8):
                        if ob >= 2:
                            tn.wait_ge(sem["VE"], ve_o0 + ob - 1)
                        for cb in range(16):
                            mi = tn.matmul(psI[ob % 2][:, 0:TC],
                                           Wsh[:, cb * 1024 + ob * 128:
                                               cb * 1024 + (ob + 1) * 128],
                                           ygb[:, cb, :],
                                           start=(cb == 0), stop=(cb == 15))
                        mi.then_inc(sem["PE"], 1)
                pe_op_prev = pe_o0 + 8

                @block.vector
                def _(v, pe_o0=pe_o0):
                    for ob in range(8):
                        v.wait_ge(sem["PE"], pe_o0 + ob + 1)
                        v.scalar_tensor_tensor(XC[:, ob, :], XC[:, ob, :], 2.0,
                                               psI[ob % 2][:, 0:TC],
                                               ALU.mult, ALU.add).then_inc(sem["VE"], 1)
                ve_resid = ve_o0 + 8

                ve_ln = emit_ln(pb, ln_ctx, XC, 8, TC, pre=[("VE", ve_resid)])
                ve_ln_prev = ve_ln

                @block.sync
                def _(sy, si=si, t0=t0, ve_ln=ve_ln):
                    sy.wait_ge(sem["ST"], pb.cnt["ST"])
                    sy.wait_ge(sem["VE"], ve_ln)
                    sy.dma_start(out=hbm[xout_name][si, :, t0:t0 + TC]
                                 .rearrange("(c p) n -> p c n", p=128),
                                 in_=XC[:, :, :]).then_inc(sem["ST"], 16)
                st_x3 = pb.bump("ST", 16)

        @block.sync
        def _(sy):
            sy.wait_ge(sem["ST"], pb.cnt["ST"])


PHASE_EMIT["C"] = emit_phaseC





def emit_phaseD(pb, cfg, hbm, xin_name, xout_name):
    nc, block, sem = pb.nc, pb.block, pb.sem
    S, T, TTD = cfg.S, cfg.T, cfg.TTD
    THN = TTD // 512
    TPS = T // TTD
    NT = S * TPS
    start = pb.snap()
    index = []
    nmat = 0
    for L in range(8):
        for lpb in range(2):
            gset = sorted({((2 ** L - 1) + ((lpb * 128 + m) >> (8 - L))) // 128
                           for m in range(128)})
            ents = []
            for gpb in gset:
                ents.append((gpb, nmat))
                nmat += 1
            index.append((L, lpb, ents))

    with ExitStack() as st:
        x3b = sb(st, nc, "dx3b", [128, 8, TTD], BF16)
        xr = sb(st, nc, "dxr", [128, 8, TTD], F32)
        qb = sb(st, nc, "dqb", [128, 8, TTD], BF16)
        gates = sb(st, nc, "dgates", [128, 2, TTD], F32)
        lp = sb(st, nc, "dlp", [128, 2, TTD], F32)
        lpb_ = sb(st, nc, "dlpb", [128, 2, TTD], BF16)
        retb = sb(st, nc, "dretb", [128, 8, TTD], BF16)
        fsb = sb(st, nc, "dfsb", [128, 2, 512], F32)
        Wq2S = sb(st, nc, "dWq2", [128, 8, D], BF16)
        RtS = sb(st, nc, "dRt", [128, 8, 256], BF16)
        LVS = sb(st, nc, "dLV", [128, 2, D], BF16)
        Wo2S = sb(st, nc, "dWo2", [128, 8, D], BF16)
        pselS = sb(st, nc, "dpsel", [128, nmat, 128], F32)
        sgnS = sb(st, nc, "dsgn", [128, 2, 8], F32)
        biaS = sb(st, nc, "dbia", [128, 2, 8], F32)
        wob = sb(st, nc, "dwob", [128, 8], F32)
        t1 = sb(st, nc, "dt1", [128, 512], F32)
        t2 = sb(st, nc, "dt2", [128, 512], F32)
        rows = sb(st, nc, "drows", [1, 3, 512], F32)
        onec = sb(st, nc, "donec", [128, 1], F32)
        oner = sb(st, nc, "doner", [1, 128], F32)
        epsln = sb(st, nc, "depsln", [1, 1], F32)
        psQ = [ps(st, nc, f"dpsQ{i}", [128, 512]) for i in range(2)]
        psF = [ps(st, nc, f"dpsF{i}", [128, 512]) for i in range(2)]

        ln_ctx = dict(onec=onec, oner=oner, psA=psQ[0], psB=psQ[1],
                      t1=t1, t2=t2, r0=rows[:, 0, :], r1=rows[:, 1, :],
                      r2=rows[:, 2, :], eps_ap=epsln)

        @block.sync
        def _(sy):
            pb.barrier(sy, start)
            sy.wait_ge(sem["LD"], pb.cnt["LD"])
            for dst, nm in ((onec, "ones_col"), (oner, "ones_row"),
                            (epsln, "eps_ln"), (pselS, "psel"),
                            (sgnS, "sgn_tab"), (biaS, "bias_tab"),
                            (wob, "Woutb_fm")):
                sy.dma_start(out=dst[:], in_=hbm[nm]).then_inc(sem["LD"], 16)
        ld_consts = pb.bump("LD", 16 * 7)

        @block.gpsimd
        def _(g):
            pb.barrier(g, start)
            g.wait_ge(sem["LG"], pb.cnt["LG"])
            g.dma_start(out=Wq2S[:], in_=r128(hbm["Wq2T"])).then_inc(sem["LG"], 16)
            g.dma_start(out=RtS[:], in_=r128(hbm["routersT"])).then_inc(sem["LG"], 16)
            g.dma_start(out=LVS[:], in_=r128(hbm["LV"])).then_inc(sem["LG"], 16)
            g.dma_start(out=Wo2S[:], in_=r128(hbm["Wout2T"])).then_inc(sem["LG"], 16)
        lg_w = pb.bump("LG", 16 * 4)

        st_y = 0
        ve_ln_last = 0

        for tile in range(NT):
            si = tile // TPS
            t0 = (tile % TPS) * TTD

            @block.sync
            def _(sy, si=si, t0=t0, st_y=st_y):
                sy.wait_ge(sem["LD"], pb.cnt["LD"])
                if st_y:
                    sy.wait_ge(sem["ST"], st_y)
                if ve_ln_last:
                    sy.wait_ge(sem["VE"], ve_ln_last)
                sy.dma_start(out=xr[:], in_=hbm[xin_name][si, :, t0:t0 + TTD]
                             .rearrange("(c p) n -> p c n", p=128)).then_inc(sem["LD"], 16)
            ld_xr = pb.bump("LD", 16)

            @block.gpsimd
            def _(g, si=si, t0=t0):
                g.wait_ge(sem["LG"], pb.cnt["LG"])
                g.wait_ge(sem["PE"], pb.cnt["PE"])
                g.dma_start(out=x3b[:], in_=hbm[xin_name][si, :, t0:t0 + TTD]
                            .rearrange("(c p) n -> p c n", p=128)).then_inc(sem["LG"], 16)
            lg_x3 = pb.bump("LG", 16)

            pe_q0 = pb.reserve("PE", 8 * THN)
            se_q0 = pb.reserve("SE", 8 * THN)

            @block.tensor
            def _(tn, lg_x3=lg_x3, pe_q0=pe_q0, se_q0=se_q0, tile=tile):
                tn.wait_ge(sem["LG"], lg_x3)
                if tile == 0:
                    tn.wait_ge(sem["LG"], lg_w)
                if ve_ln_last:
                    tn.wait_ge(sem["VE"], ve_ln_last)
                for ob in range(8):
                    for th in range(THN):
                        idx = ob * THN + th
                        if idx >= 2:
                            tn.wait_ge(sem["SE"], se_q0 + idx - 1)
                        for cb in range(8):
                            mi = tn.matmul(psQ[idx % 2][:, :],
                                           Wq2S[:, cb, ob * 128:(ob + 1) * 128],
                                           x3b[:, cb, th * 512:(th + 1) * 512],
                                           start=(cb == 0), stop=(cb == 7))
                        mi.then_inc(sem["PE"], 1)

            @block.scalar
            def _(s_, pe_q0=pe_q0, tile=tile):
                if tile == 0:
                    s_.wait_ge(sem["LD"], ld_consts)
                for idx in range(8 * THN):
                    s_.wait_ge(sem["PE"], pe_q0 + idx + 1)
                    ob, th = idx // THN, idx % THN
                    s_.copy(qb[:, ob, th * 512:(th + 1) * 512],
                            psQ[idx % 2][:, :]).then_inc(sem["SE"], 1)
            se_qb = se_q0 + 8 * THN

            pe_g0 = pb.reserve("PEb", 2 * THN)
            se_g0 = pb.reserve("SE", 2 * THN)

            @block.tensor
            def _(tn, pe_g0=pe_g0, se_g0=se_g0, se_qb=se_qb):
                tn.wait_ge(sem["SE"], se_qb)
                for gpb in range(2):
                    for th in range(THN):
                        idx = gpb * THN + th
                        if idx >= 2:
                            tn.wait_ge(sem["SE"], se_g0 + idx - 1)
                        for cb in range(8):
                            mi = tn.matmul(psF[idx % 2][:, :],
                                           RtS[:, cb, gpb * 128:(gpb + 1) * 128],
                                           qb[:, cb, th * 512:(th + 1) * 512],
                                           start=(cb == 0), stop=(cb == 7))
                        mi.then_inc(sem["PEb"], 1)

            @block.scalar
            def _(s_, pe_g0=pe_g0, se_g0=se_g0):
                for idx in range(2 * THN):
                    s_.wait_ge(sem["PEb"], pe_g0 + idx + 1)
                    gpb, th = idx // THN, idx % THN
                    s_.activation(gates[:, gpb, th * 512:(th + 1) * 512],
                                  psF[idx % 2][:, :], AF.Sigmoid).then_inc(sem["SE"], 1)
            se_gates = se_g0 + 2 * THN

            nstep = 0
            pe_f0 = pb.reserve("PEb", 16 * THN)
            se_f0 = pb.reserve("SE", 16 * THN)
            ve_f0 = pb.reserve("VE", 16 * THN)

            @block.tensor
            def _(tn, pe_f0=pe_f0, se_f0=se_f0, se_gates=se_gates, ve_f0=ve_f0):
                tn.wait_ge(sem["SE"], se_gates)
                k = 0
                for (L, lpb, ents) in index:
                    for th in range(THN):
                        if k >= 2:
                            tn.wait_ge(sem["SE"], se_f0 + k - 1)
                        for j, (gpb, mi_) in enumerate(ents):
                            mm = tn.matmul(psF[k % 2][:, :], pselS[:, mi_, :],
                                           gates[:, gpb, th * 512:(th + 1) * 512],
                                           start=(j == 0), stop=(j == len(ents) - 1))
                        mm.then_inc(sem["PEb"], 1)
                        k += 1

            @block.scalar
            def _(s_, pe_f0=pe_f0, se_f0=se_f0, ve_f0=ve_f0):
                k = 0
                for (L, lpb, ents) in index:
                    for th in range(THN):
                        s_.wait_ge(sem["PEb"], pe_f0 + k + 1)
                        if k >= 2:
                            s_.wait_ge(sem["VE"], ve_f0 + k - 1)
                        s_.activation(fsb[:, k % 2, :], psF[k % 2][:, :],
                                      AF.Identity, scale=sgnS[:, lpb, L:L + 1],
                                      bias=biaS[:, lpb, L:L + 1]).then_inc(sem["SE"], 1)
                        k += 1

            @block.vector
            def _(v, se_f0=se_f0, ve_f0=ve_f0):
                k = 0
                for (L, lpb, ents) in index:
                    for th in range(THN):
                        v.wait_ge(sem["SE"], se_f0 + k + 1)
                        sl = lp[:, lpb, th * 512:(th + 1) * 512]
                        if L == 0:
                            v.tensor_copy(sl, fsb[:, k % 2, :]).then_inc(sem["VE"], 1)
                        else:
                            if k >= 2 * THN:
                                v.wait_ge(sem["VE"], ve_f0 + k - 2 * THN + 1)
                            v.tensor_mul(sl, sl, fsb[:, k % 2, :]).then_inc(sem["VE"], 1)
                        k += 1
            ve_lp = ve_f0 + 16 * THN

            ve_lpb = pb.reserve("VE", 1)

            @block.vector
            def _(v, ve_lp=ve_lp):
                v.wait_ge(sem["VE"], ve_lp)
                v.tensor_copy(lpb_[:, :, :], lp[:, :, :]).then_inc(sem["VE"], 1)

            pe_r0 = pb.reserve("PE", 8 * THN)
            se_r0 = pb.reserve("SE", 8 * THN)

            @block.tensor
            def _(tn, pe_r0=pe_r0, se_r0=se_r0, ve_lpb=ve_lpb, se_qb=se_qb):
                tn.wait_ge(sem["VE"], ve_lpb + 1)
                for ob in range(8):
                    for th in range(THN):
                        idx = ob * THN + th
                        if idx >= 2:
                            tn.wait_ge(sem["SE"], se_r0 + idx - 1)
                        for cb in range(2):
                            mi = tn.matmul(psQ[idx % 2][:, :],
                                           LVS[:, cb, ob * 128:(ob + 1) * 128],
                                           lpb_[:, cb, th * 512:(th + 1) * 512],
                                           start=(cb == 0), stop=(cb == 1))
                        mi.then_inc(sem["PE"], 1)

            @block.scalar
            def _(s_, pe_r0=pe_r0):
                for idx in range(8 * THN):
                    s_.wait_ge(sem["PE"], pe_r0 + idx + 1)
                    ob, th = idx // THN, idx % THN
                    s_.copy(retb[:, ob, th * 512:(th + 1) * 512],
                            psQ[idx % 2][:, :]).then_inc(sem["SE"], 1)
            se_ret = se_r0 + 8 * THN

            pe_w0 = pb.reserve("PEb", 8 * THN)
            ve_w0 = pb.reserve("VE", 8 * THN)

            @block.tensor
            def _(tn, pe_w0=pe_w0, ve_w0=ve_w0, se_ret=se_ret):
                tn.wait_ge(sem["SE"], se_ret)
                for ob in range(8):
                    for th in range(THN):
                        idx = ob * THN + th
                        if idx >= 2:
                            tn.wait_ge(sem["VE"], ve_w0 + idx - 1)
                        for cb in range(8):
                            mi = tn.matmul(psF[idx % 2][:, :],
                                           Wo2S[:, cb, ob * 128:(ob + 1) * 128],
                                           retb[:, cb, th * 512:(th + 1) * 512],
                                           start=(cb == 0), stop=(cb == 7))
                        mi.then_inc(sem["PEb"], 1)

            @block.vector
            def _(v, pe_w0=pe_w0, ld_xr=ld_xr):
                v.wait_ge(sem["LD"], ld_xr)
                for ob in range(8):
                    for th in range(THN):
                        idx = ob * THN + th
                        v.wait_ge(sem["PEb"], pe_w0 + idx + 1)
                        v.scalar_tensor_tensor(xr[:, ob, th * 512:(th + 1) * 512],
                                               psF[idx % 2][:, :], wob[:, ob:ob + 1],
                                               xr[:, ob, th * 512:(th + 1) * 512],
                                               ALU.add, ALU.add).then_inc(sem["VE"], 1)
            ve_out = ve_w0 + 8 * THN

            ve_lns = []
            ve_pre = ve_out
            for th in range(THN):
                xslice = xr[:, :, th * 512:(th + 1) * 512]
                ve_l = emit_ln(pb, ln_ctx, xslice, 8, 512,
                               pre=[("VE", ve_pre)])
                ve_lns.append(ve_l)
                ve_pre = ve_l
            ve_ln_last = ve_lns[-1]

            @block.sync
            def _(sy, si=si, t0=t0, ve_lns=ve_lns):
                sy.wait_ge(sem["ST"], pb.cnt["ST"])
                for vv in ve_lns:
                    sy.wait_ge(sem["VE"], vv)
                sy.dma_start(out=hbm[xout_name][si, :, t0:t0 + TTD]
                             .rearrange("(c p) n -> p c n", p=128),
                             in_=xr[:, :, :]).then_inc(sem["ST"], 16)
            st_y = pb.bump("ST", 16)

        @block.sync
        def _(sy):
            sy.wait_ge(sem["ST"], pb.cnt["ST"])


PHASE_EMIT["D"] = emit_phaseD



PHASE_IO = {"A": ("x", "x1"), "B": ("x1", "x2"), "C": ("x2", "x3"), "D": ("x3", "y")}


def build_program(cfg, phases, arrs):
    nc = bass.Bass("TRN2", target_bir_lowering=False)
    hbm = {}
    S, T = cfg.S, cfg.T
    first_in = PHASE_IO[phases[0]][0]
    last_out = PHASE_IO[phases[-1]][1]
    act_names = [first_in] + [PHASE_IO[p][1] for p in phases]
    for nm in act_names:
        if nm == first_in:
            kind = "ExternalInput"
        elif nm == last_out:
            kind = "ExternalOutput"
        else:
            kind = "Internal"
        hbm[nm] = nc.dram_tensor(nm, [S, D, T], F32, kind=kind).ap()
    if "C" in phases:
        hbm["z_spill"] = nc.dram_tensor("z_spill", [S, 16, 128, T], F32,
                                        kind="Internal").ap()
    for name, a in arrs.items():
        if name.startswith("_"):
            continue
        hbm[name] = nc.dram_tensor(name, list(a.shape), F32,
                                   kind="ExternalInput").ap()
    with ExitStack() as stack:
        pb = PB(nc, stack, None)
        block = stack.enter_context(nc.Block())
        pb.block = block
        for p in phases:
            PHASE_EMIT[p](pb, cfg, hbm, PHASE_IO[p][0], PHASE_IO[p][1])
    return nc


PHASE_EMIT["A"] = emit_phaseA


def make_arrs(inputs, cfg):
    consts = make_consts(cfg)
    w = prep_weights(inputs)
    arrs = {k: v for k, v in consts.items() if not k.startswith("_")}
    arrs.update(w)
    return arrs, consts["_psel_index"]


KERNEL_GROUPS = ["ABCD"]


def kernel(**inputs):
    _ = inputs.get("L1_ffn_leaf_keys")
    cfg = Cfg()
    arrs, psel_index = make_arrs(inputs, cfg)
    x = inputs["L0_attn_x"]
    B = x.shape[0]
    shards = [np.ascontiguousarray(x[2 * c:2 * c + 2].transpose(0, 2, 1))
              for c in range(N_CORES)]
    y = np.empty((B, cfg.T, D), np.float32)
    cur = shards
    for grp in KERNEL_GROUPS:
        nc = build_program(cfg, grp, arrs)
        iname = PHASE_IO[grp[0]][0]
        oname = PHASE_IO[grp[-1]][1]
        in_maps = [dict(arrs, **{iname: cur[c]}) for c in range(N_CORES)]
        res = run_bass_kernel_spmd(nc, in_maps, core_ids=list(range(N_CORES)))
        cur = [np.asarray(r[oname]) for r in res.results]
    for c in range(N_CORES):
        y[2 * c:2 * c + 2] = cur[c].transpose(0, 2, 1)
    return y


# version 09
# speedup vs baseline: 1.1184x; 1.1184x over previous
import math
from contextlib import ExitStack

import numpy as np
import ml_dtypes

import concourse.bass as bass
import concourse.mybir as mybir
from concourse.bass_utils import run_bass_kernel_spmd

F32 = mybir.dt.float32
BF16 = mybir.dt.bfloat16
AF = mybir.ActivationFunctionType
ALU = mybir.AluOpType
AX = mybir.AxisListType

N_CORES = 8
D = 1024
NH, DH, WIN = 16, 64, 256
I_FF, NE = 2048, 8
DI, NSTATE, DTR, KCONV = 2048, 16, 64, 4
M_LEAF = 256
LN_EPS = 1e-5
RMS_EPS = 1.1920928955078125e-07
CSHIFT = 24.0


class Cfg:
    def __init__(self, S=2, T=2048, slab=256, ttb=1024, tc=256, ttd=1024):
        self.S, self.T, self.SLAB, self.TTB, self.TC, self.TTD = S, T, slab, ttb, tc, ttd
        assert T % slab == 0 and slab % 128 == 0 and slab >= 256
        assert T % ttb == 0 and T % tc == 0 and T % ttd == 0 and tc % 128 == 0



def make_consts(cfg):
    T = cfg.T
    inv = 1.0 / 10000.0 ** (np.arange(0, DH, 2) / DH)
    fr = np.outer(np.arange(T), inv)
    cos, sin = np.cos(fr), np.sin(fr)
    p = np.arange(128)
    j = p % 64
    jj = j % 32
    sgn = np.where(j < 32, -1.0, 1.0)
    cos_pt = cos[:, jj].T
    sin_pt = (sin[:, jj] * sgn[None, :]).T
    c = {}
    c["costab_q"] = np.ascontiguousarray(cos_pt / math.sqrt(DH)).astype(np.float32)
    c["sintab_q"] = np.ascontiguousarray(sin_pt / math.sqrt(DH)).astype(np.float32)
    c["costab_k"] = np.ascontiguousarray(cos_pt).astype(np.float32)
    c["sintab_k"] = np.ascontiguousarray(sin_pt).astype(np.float32)
    q = np.arange(128)[None, :]
    k = np.arange(128)[:, None]
    c["maskT"] = np.ascontiguousarray(np.stack(
        [(q < k), np.ones((128, 128), bool), (k <= q)], 1)).astype(np.float32)
    c["ident"] = np.eye(128, dtype=np.float32)
    c["ones_col"] = np.ones((128, 1), np.float32)
    c["ones_row"] = np.ones((1, 128), np.float32)
    c["cshift_col"] = np.full((128, 1), -CSHIFT, np.float32)
    c["eps_ln"] = np.full((1, 1), LN_EPS, np.float32)
    c["eps_rms"] = np.full((1, 1), RMS_EPS, np.float32)
    sel8 = np.zeros((8, 8, 128), np.float32)
    for e in range(8):
        sel8[e, e, :] = 1.0
    c["sel8"] = sel8
    esel = np.zeros((32, 32, 128), np.float32)
    for n in range(32):
        esel[n, n, :] = 1.0
    c["esel"] = esel
    c["esel_cols"] = np.eye(32, dtype=np.float32)
    mats, index = [], []
    for L in range(8):
        for lpb in range(2):
            byg = {}
            for mloc in range(128):
                mg = lpb * 128 + mloc
                node = (2 ** L - 1) + (mg >> (8 - L))
                gpb, r = node // 128, node % 128
                byg.setdefault(gpb, np.zeros((128, 128), np.float32))[r, mloc] = 1.0
            for gpb, m in sorted(byg.items()):
                index.append((L, lpb, gpb, len(mats)))
                mats.append(m)
    c["psel"] = np.ascontiguousarray(np.stack(mats, 1))
    PSEL_INDEX = index
    c["_psel_index"] = index
    sgn_t = np.zeros((128, 2, 8), np.float32)
    bia_t = np.zeros((128, 2, 8), np.float32)
    for L in range(8):
        for lpb in range(2):
            for mloc in range(128):
                b = ((lpb * 128 + mloc) >> (7 - L)) & 1
                sgn_t[mloc, lpb, L] = 2.0 * b - 1.0
                bia_t[mloc, lpb, L] = 1.0 - b
    c["sgn_tab"] = sgn_t
    c["bias_tab"] = bia_t
    return c


def prep_weights(inp):
    A = np.ascontiguousarray
    w = {}
    wqkv = inp["L0_attn_w_qkv"]
    w["WqT"], w["WkT"], w["WvT"] = (A(wqkv[i * D:(i + 1) * D].T) for i in range(3))
    w["WoT"] = A(inp["L0_attn_w_out"].T)
    w["WsgT"] = A(inp["L0_ffn_sg_w"].T)
    w["WsuT"] = A(inp["L0_ffn_su_w"].T)
    w["WsdT"] = A(inp["L0_ffn_sd_w"].T)
    w["routerT"] = A(inp["L0_ffn_router_w"].T)
    w["router_b"] = A(inp["L0_ffn_routing_bias"].reshape(8, 1))
    w["WgT"] = A(inp["L0_ffn_wg"].transpose(0, 2, 1))
    w["WuT"] = A(inp["L0_ffn_wu"].transpose(0, 2, 1))
    w["WdT"] = A(inp["L0_ffn_wd"].transpose(0, 2, 1))
    w["normw_fm"] = A(inp["L1_attn_norm_w"].reshape(16, 64).reshape(8, 128).T)
    w["normw_fm"] = A(inp["L1_attn_norm_w"].reshape(8, 128).T)
    w["WipT"] = A(inp["L1_attn_in_proj_w"].T)
    w["convw_fm"] = A(inp["L1_attn_conv_w"].reshape(16, 128, KCONV).transpose(1, 0, 2))
    w["convb_fm"] = A(inp["L1_attn_conv_b"].reshape(16, 128).T)
    w["WxpT"] = A(inp["L1_attn_x_proj_w"].T)
    w["WdtT"] = A(inp["L1_attn_dt_proj_w"].T)
    w["dtb_fm"] = A(inp["L1_attn_dt_proj_b"].reshape(16, 128).T)
    w["Alog_fm"] = A(inp["L1_attn_A_log"].reshape(16, 128, NSTATE).transpose(1, 0, 2))
    w["D_fm"] = A(inp["L1_attn_D"].reshape(16, 128).T)
    w["WopT"] = A(inp["L1_attn_out_proj_w"].T)
    w["Wq2T"] = A(inp["L1_ffn_Wq"].T)
    w["routersT"] = A(np.pad(inp["L1_ffn_routers"].T, ((0, 0), (0, 1))))
    w["LV"] = A(inp["L1_ffn_leaf_values"])
    w["Wout2T"] = A(inp["L1_ffn_Wout_w"].T)
    w["Woutb_fm"] = A(inp["L1_ffn_Wout_b"].reshape(8, 128).T)
    return w



SEMS = ("LD", "LG", "PE", "VE", "SE", "ST", "PEa", "PEb", "VEa", "VEb", "GP")


class PB:
    def __init__(self, nc, stack, block):
        self.nc, self.stack, self.block = nc, stack, block
        self.sem, self.cnt = {}, {}
        for name in SEMS:
            self.sem[name] = stack.enter_context(nc.semaphore(name))
            self.cnt[name] = 0

    def bump(self, name, by=1):
        self.cnt[name] += by
        return self.cnt[name]

    def reserve(self, name, k):
        base = self.cnt[name]
        self.cnt[name] += k
        return base

    def snap(self):
        return dict(self.cnt)

    def barrier(self, eng, snap):
        for name, v in snap.items():
            if v > 0:
                eng.wait_ge(self.sem[name], v)


def sb(st, nc, name, shape, dtype):
    return st.enter_context(nc.sbuf_tensor(name, shape, dtype))


def ps(st, nc, name, shape, dtype=F32):
    return st.enter_context(nc.psum_tensor(name, shape, dtype))


def r128(ap):
    return ap.rearrange("(c p) n -> p c n", p=128)


PHASE_EMIT = {}


def emit_ln(pb, ctx, xbuf, nb, W, pre=(), eps=LN_EPS):
    nc, block, sem = pb.nc, pb.block, pb.sem
    onec, oner = ctx["onec"], ctx["oner"]
    psA, psB = ctx["psA"], ctx["psB"]
    t1, t2 = ctx["t1"], ctx["t2"]
    r0, r1, r2 = ctx["r0"], ctx["r1"], ctx["r2"]
    ndim = nb * 128

    se0 = pb.reserve("SE", nb)
    pe0 = pb.reserve("PE", nb)

    @block.scalar
    def _(s):
        for nm, v in pre:
            s.wait_ge(sem[nm], v)
        s.wait_ge(sem["PE"], pe0)
        for cb in range(nb):
            if cb >= 2:
                s.wait_ge(sem["PE"], pe0 + cb - 1)
            s.activation((t1 if cb % 2 == 0 else t2)[:, 0:W], xbuf[:, cb, 0:W],
                         AF.Square).then_inc(sem["SE"], 1)

    @block.tensor
    def _(tn):
        for nm, v in pre:
            tn.wait_ge(sem[nm], v)
        for cb in range(nb):
            tn.matmul(psA[0:1, 0:W], onec[:, 0:1], xbuf[:, cb, 0:W],
                      start=(cb == 0), stop=(cb == nb - 1))
        for cb in range(nb):
            tn.wait_ge(sem["SE"], se0 + cb + 1)
            tn.matmul(psB[0:1, 0:W], onec[:, 0:1],
                      (t1 if cb % 2 == 0 else t2)[:, 0:W],
                      start=(cb == 0), stop=(cb == nb - 1)).then_inc(sem["PE"], 1)
    pe_stats = pe0 + nb

    @block.vector
    def _(v):
        v.wait_ge(sem["PE"], pe_stats)
        v.tensor_scalar_mul(r0[:, 0:W], psA[0:1, 0:W], 1.0 / ndim).then_inc(sem["VE"], 1)
    ve_m = pb.bump("VE")

    @block.scalar
    def _(s):
        s.wait_ge(sem["VE"], ve_m)
        s.activation(r1[:, 0:W], r0[:, 0:W], AF.Square).then_inc(sem["SE"], 1)
    se_msq = pb.bump("SE")

    @block.vector
    def _(v):
        v.wait_ge(sem["SE"], se_msq)
        v.scalar_tensor_tensor(r2[:, 0:W], psB[0:1, 0:W], 1.0 / ndim,
                               r1[:, 0:W], ALU.mult, ALU.subtract).then_inc(sem["VE"], 1)
    ve_var = pb.bump("VE")

    se_ln0 = pb.reserve("SE", 2)

    @block.scalar
    def _(s):
        s.wait_ge(sem["VE"], ve_var)
        s.activation(r1[:, 0:W], r2[:, 0:W], AF.Ln,
                     bias=ctx["eps_ap"][:]).then_inc(sem["SE"], 1)
        s.wait_ge(sem["SE"], se_ln0 + 1)
        s.activation(r2[:, 0:W], r1[:, 0:W], AF.Exp,
                     scale=-0.5).then_inc(sem["SE"], 1)
    se_rstd = se_ln0 + 2

    @block.tensor
    def _(tn):
        tn.wait_ge(sem["SE"], se_rstd)
        tn.matmul(psA[:, 0:W], oner[0:1, :], r0[:, 0:W], start=True, stop=True)
        tn.matmul(psB[:, 0:W], oner[0:1, :], r2[:, 0:W],
                  start=True, stop=True).then_inc(sem["PE"], 1)
    pe_bc = pb.bump("PE")

    ve0 = pb.reserve("VE", 2 * nb)

    @block.vector
    def _(v):
        v.wait_ge(sem["PE"], pe_bc)
        for cb in range(nb):
            v.tensor_sub(xbuf[:, cb, 0:W], xbuf[:, cb, 0:W],
                         psA[:, 0:W]).then_inc(sem["VE"], 1)
            v.wait_ge(sem["VE"], ve0 + 2 * cb + 1)
            v.tensor_mul(xbuf[:, cb, 0:W], xbuf[:, cb, 0:W],
                         psB[:, 0:W]).then_inc(sem["VE"], 1)
    return ve0 + 2 * nb



def emit_phaseA(pb, cfg, hbm, xin_name, xout_name):
    nc, block, sem = pb.nc, pb.block, pb.sem
    S, T, SLAB = cfg.S, cfg.T, cfg.SLAB
    assert 256 <= SLAB <= 512
    QTS = SLAB // 128
    NSLAB = T // SLAB
    start = pb.snap()

    with ExitStack() as st:
        XS = sb(st, nc, "aXS", [128, 8, SLAB], F32)
        XSb = sb(st, nc, "aXSb", [128, 8, SLAB], BF16)
        Qf = sb(st, nc, "aQf", [128, 8, SLAB], F32)
        Qs = sb(st, nc, "aQs", [128, 8, SLAB], F32)
        qrot = sb(st, nc, "aqrot", [128, 8, SLAB], BF16)
        krot = sb(st, nc, "akrot", [128, 2, 8, SLAB], BF16)
        vsb = sb(st, nc, "avsb", [128, 2, QTS, NH, 66], BF16)
        PTraw = sb(st, nc, "aPTraw", [128, 2, 3, 128], BF16)
        PTmsk = sb(st, nc, "aPTmsk", [128, 2, 3, 128], BF16)
        att = sb(st, nc, "aatt", [128, NH * DH], BF16)
        afm = sb(st, nc, "aafm", [128, 8, SLAB], BF16)
        xo = sb(st, nc, "axo", [128, 8, SLAB], F32)
        rinv = sb(st, nc, "arinv", [128, 2], F32)
        t1 = sb(st, nc, "at1", [128, 512], F32)
        t2 = sb(st, nc, "at2", [128, 512], F32)
        rt1 = sb(st, nc, "art1", [128, SLAB], F32)
        rt2 = sb(st, nc, "art2", [128, SLAB], F32)
        rows = sb(st, nc, "arows", [1, 3, 512], F32)
        cosq = sb(st, nc, "acosq", [128, SLAB], F32)
        sinq = sb(st, nc, "asinq", [128, SLAB], F32)
        cosk = sb(st, nc, "acosk", [128, SLAB], F32)
        sink = sb(st, nc, "asink", [128, SLAB], F32)
        maskT = sb(st, nc, "amaskT", [128, 3, 128], BF16)
        identb = sb(st, nc, "aidentb", [128, 128], BF16)
        onec = sb(st, nc, "aonec", [128, 1], F32)
        oner = sb(st, nc, "aoner", [1, 128], F32)
        cshift = sb(st, nc, "acshift", [128, 1], F32)
        epsln = sb(st, nc, "aepsln", [1, 1], F32)
        WqTs = sb(st, nc, "aWqT", [128, 8, D], BF16)
        WkTs = sb(st, nc, "aWkT", [128, 8, D], BF16)
        WvTs = sb(st, nc, "aWvT", [128, 8, D], BF16)
        WoTs = sb(st, nc, "aWoT", [128, 8, D], BF16)
        psE = [ps(st, nc, f"apsE{i}", [128, 512]) for i in range(2)]
        psS = [ps(st, nc, f"apsS{i}", [128, 3, 128]) for i in range(2)]
        psO = [ps(st, nc, f"apsO{i}", [128, 66]) for i in range(2)]
        psTR = [ps(st, nc, f"apsTR{i}", [128, 128], BF16) for i in range(2)]

        ln_ctx = dict(onec=onec, oner=oner, psA=psE[0], psB=psE[1],
                      t1=t1, t2=t2, r0=rows[:, 0, :], r1=rows[:, 1, :],
                      r2=rows[:, 2, :], eps_ap=epsln)

        @block.sync
        def _(sy):
            pb.barrier(sy, start)
            sy.dma_start(out=onec[:], in_=hbm["ones_col"]).then_inc(sem["LD"], 16)
            sy.dma_start(out=oner[:], in_=hbm["ones_row"]).then_inc(sem["LD"], 16)
            sy.dma_start(out=cshift[:], in_=hbm["cshift_col"]).then_inc(sem["LD"], 16)
            sy.dma_start(out=epsln[:], in_=hbm["eps_ln"]).then_inc(sem["LD"], 16)
        ld_consts = pb.bump("LD", 16 * 4)

        @block.gpsimd
        def _(g):
            pb.barrier(g, start)
            for nm, dst in (("WqT", WqTs), ("WkT", WkTs), ("WvT", WvTs), ("WoT", WoTs)):
                g.dma_start(out=dst[:], in_=r128(hbm[nm])).then_inc(sem["LG"], 16)
            g.dma_start(out=identb[:], in_=hbm["ident"]).then_inc(sem["LG"], 16)
            g.dma_start(out=maskT[:], in_=hbm["maskT"]).then_inc(sem["LG"], 16)
        lg_w = pb.bump("LG", 16 * 6)

        st_x1 = 0
        ve_xo_resid = 0
        ve_rope_k_prev = 0
        pe_v_last = 0
        ve_ln_prev = 0
        pe_tr_prev = None

        for s in range(S):
            for jjj in range(NSLAB):
                par = jjj % 2
                t0 = jjj * SLAB
                first = (s == 0 and jjj == 0)

                @block.sync
                def _(sy):
                    sy.wait_ge(sem["LD"], pb.cnt["LD"])
                    if ve_xo_resid:
                        sy.wait_ge(sem["VE"], ve_xo_resid)
                    if ve_rope_k_prev:
                        sy.wait_ge(sem["VE"], ve_rope_k_prev)
                    sy.dma_start(out=XS[:], in_=hbm[xin_name][s, :, t0:t0 + SLAB]
                                 .rearrange("(c p) n -> p c n", p=128)).then_inc(sem["LD"], 16)
                    for dst, nm in ((cosq, "costab_q"), (sinq, "sintab_q"),
                                    (cosk, "costab_k"), (sink, "sintab_k")):
                        sy.dma_start(out=dst[:], in_=hbm[nm][:, t0:t0 + SLAB]).then_inc(sem["LD"], 16)
                ld_slab = pb.bump("LD", 16 * 5)

                @block.vector
                def _(v):
                    v.wait_ge(sem["LD"], ld_slab)
                    if first:
                        v.wait_ge(sem["LG"], lg_w)
                        v.wait_ge(sem["LD"], ld_consts)
                    if pe_v_last:
                        v.wait_ge(sem["PE"], pe_v_last)
                    v.tensor_copy(XSb[:], XS[:]).then_inc(sem["VE"], 1)
                ve_xsb = pb.bump("VE")

                ve_rope = {}
                for which in ("q", "k"):
                    Wsrc = WqTs if which == "q" else WkTs
                    ctab = cosq if which == "q" else cosk
                    stab = sinq if which == "q" else sink
                    pe_qk = pb.reserve("PE", 8)
                    ve_ev = pb.reserve("VE", 8)

                    @block.tensor
                    def _(tn, Wsrc=Wsrc, pe_qk=pe_qk, ve_ev=ve_ev, which=which):
                        tn.wait_ge(sem["VE"], ve_xsb)
                        if which == "q" and ve_ln_prev:
                            tn.wait_ge(sem["VE"], ve_ln_prev)
                        if which == "k":
                            tn.wait_ge(sem["VE"], ve_rope["q"])
                        for ob in range(8):
                            if ob >= 2:
                                tn.wait_ge(sem["VE"], ve_ev + ob - 1)
                            for cb in range(8):
                                mi = tn.matmul(psE[ob % 2][:, 0:SLAB],
                                               Wsrc[:, cb, ob * 128:(ob + 1) * 128],
                                               XSb[:, cb, :],
                                               start=(cb == 0), stop=(cb == 7))
                            mi.then_inc(sem["PE"], 1)

                    @block.vector
                    def _(v, pe_qk=pe_qk, ve_ev=ve_ev):
                        for ob in range(8):
                            v.wait_ge(sem["PE"], pe_qk + ob + 1)
                            v.tensor_copy(Qf[:, ob, :],
                                          psE[ob % 2][:, 0:SLAB]).then_inc(sem["VE"], 1)

                    @block.sync
                    def _(sy, ve_ev=ve_ev):
                        sy.wait_ge(sem["LD"], pb.cnt["LD"])
                        sy.wait_ge(sem["VE"], ve_ev + 8)
                        for (d0, sc0) in ((0, 32), (32, 0), (64, 96), (96, 64)):
                            sy.dma_start(out=Qs[d0:d0 + 32, :, :],
                                         in_=Qf[sc0:sc0 + 32, :, :]).then_inc(sem["LD"], 16)
                    ld_swap = pb.bump("LD", 16 * 4)

                    ve_r0 = pb.reserve("VE", 2 * 8)
                    pea_total = pb.cnt["PEa"]

                    @block.vector
                    def _(v, ctab=ctab, stab=stab, which=which, ve_r0=ve_r0,
                          ld_swap=ld_swap, par=par, pea_total=pea_total):
                        v.wait_ge(sem["LD"], ld_swap)
                        if pea_total:
                            v.wait_ge(sem["PEa"], pea_total)
                        for ob in range(8):
                            dst = (qrot[:, ob, :] if which == "q"
                                   else krot[:, par, ob, :])
                            if ob >= 1:
                                v.wait_ge(sem["VE"], ve_r0 + 2 * ob)
                            v.tensor_mul(rt1[:, :], Qf[:, ob, :], ctab[:, :])
                            v.tensor_mul(rt2[:, :], Qs[:, ob, :], stab[:, :]).then_inc(sem["VE"], 1)
                            v.wait_ge(sem["VE"], ve_r0 + 2 * ob + 1)
                            v.tensor_add(dst, rt1[:, :], rt2[:, :]).then_inc(sem["VE"], 1)
                    ve_rope[which] = ve_r0 + 16

                pe_v0 = pb.reserve("PE", 2 * QTS)
                ve_v0 = pb.reserve("VE", 2 * QTS)
                peb_total = pb.cnt["PEb"]

                @block.tensor
                def _(tn, pe_v0=pe_v0, ve_v0=ve_v0):
                    tn.wait_ge(sem["VE"], ve_rope["k"])
                    for tc in range(QTS):
                        for hf in range(2):
                            if tc * 2 + hf >= 2:
                                tn.wait_ge(sem["VE"], ve_v0 + tc * 2 + hf - 1)
                            for cb in range(8):
                                mi = tn.matmul(psE[hf][:, :],
                                               XSb[:, cb, tc * 128:(tc + 1) * 128],
                                               WvTs[:, cb, hf * 512:(hf + 1) * 512],
                                               start=(cb == 0), stop=(cb == 7))
                            mi.then_inc(sem["PE"], 1)

                @block.vector
                def _(v, pe_v0=pe_v0, par=par, peb_total=peb_total):
                    if peb_total:
                        v.wait_ge(sem["PEb"], peb_total)
                    v.memset(vsb[:, par, :, :, 64:65], 1.0)
                    for tc in range(QTS):
                        for hf in range(2):
                            v.wait_ge(sem["PE"], pe_v0 + tc * 2 + hf + 1)
                            v.tensor_copy(
                                vsb[:, par, tc, hf * 8:(hf + 1) * 8, 0:64],
                                psE[hf][:, :].rearrange("p (h d) -> p h d", d=64)
                            ).then_inc(sem["VE"], 1)
                ve_v_done = ve_v0 + 2 * QTS

                for qt in range(QTS):
                    qg = jjj * QTS + qt
                    nv = min(3, qg + 1)
                    s0 = 3 - nv
                    pea0 = pb.reserve("PEa", NH + 8)
                    peb0 = pb.reserve("PEb", NH)
                    se0 = pb.reserve("SE", NH)
                    vea0 = pb.reserve("VEa", NH + 8)
                    veb0 = pb.reserve("VEb", 2 * NH)

                    def ktile(kti, h, qg=qg):
                        kg = qg - (2 - kti)
                        kpar = (kg // QTS) % 2
                        ktc = kg % QTS
                        ob, rb = h // 2, (h % 2) * 64
                        kr = krot[rb:rb + 64, kpar, ob, ktc * 128:(ktc + 1) * 128]
                        vv = vsb[:, kpar, ktc, h, 0:65]
                        return kr, vv

                    @block.tensor
                    def _(tn, qt=qt, qg=qg, nv=nv, s0=s0, pea0=pea0, peb0=peb0,
                          se0=se0, vea0=vea0, veb0=veb0, ktile=ktile,
                          pe_tr_prev=pe_tr_prev):
                        def emit_st(h):
                            if h < 2:
                                tn.wait_ge(sem["VE"], ve_rope["k"])
                                tn.wait_ge(sem["SE"], se0)
                            else:
                                tn.wait_ge(sem["SE"], se0 + h - 1)
                            ob, rb = h // 2, (h % 2) * 64
                            qr = qrot[rb:rb + 64, ob, qt * 128:(qt + 1) * 128]
                            for kti in range(s0, 3):
                                kr, _ = ktile(kti, h)
                                mi = tn.matmul(psS[h % 2][:, kti, :], kr, qr,
                                               start=True, stop=True)
                            mi.then_inc(sem["PEa"], 1)

                        def emit_pv(h):
                            tn.wait_ge(sem["VEa"], vea0 + h + 1)
                            if h < 2:
                                tn.wait_ge(sem["VEb"], veb0)
                                tn.wait_ge(sem["VE"], ve_v_done)
                            else:
                                tn.wait_ge(sem["VEb"], veb0 + 2 * (h - 2) + 2)
                            for kti in range(s0, 3):
                                _, vv = ktile(kti, h)
                                mi = tn.matmul(psO[h % 2][:, 0:65],
                                               PTmsk[:, h % 2, kti, :], vv,
                                               start=(kti == s0), stop=(kti == 2))
                            mi.then_inc(sem["PEb"], 1)

                        for h in range(NH):
                            emit_st(h)
                            if h >= 1:
                                emit_pv(h - 1)
                        emit_pv(NH - 1)
                        for ob in range(8):
                            tn.wait_ge(sem["VEb"], veb0 + 2 * (2 * ob + 1) + 2)
                            if ob >= 2:
                                tn.wait_ge(sem["VEa"], vea0 + NH + ob - 1)
                            tn.transpose(psTR[ob % 2][:, :],
                                         att[:, ob * 128:(ob + 1) * 128],
                                         identb[:, :]).then_inc(sem["PEa"], 1)

                    @block.scalar
                    def _(s_, nv=nv, s0=s0, pea0=pea0, se0=se0, vea0=vea0):
                        if s == 0 and jjj == 0 and qt == 0:
                            s_.wait_ge(sem["LD"], ld_consts)
                        for h in range(NH):
                            s_.wait_ge(sem["PEa"], pea0 + h + 1)
                            if h < 2:
                                s_.wait_ge(sem["VEa"], vea0)
                            else:
                                s_.wait_ge(sem["VEa"], vea0 + h - 1)
                            s_.activation(PTraw[:, h % 2, s0:3, :],
                                          psS[h % 2][:, s0:3, :],
                                          AF.Exp, bias=cshift[:, 0:1]).then_inc(sem["SE"], 1)

                    @block.vector
                    def _(v, nv=nv, s0=s0, qt=qt, se0=se0, peb0=peb0, vea0=vea0,
                          veb0=veb0, pea0=pea0, pe_tr_prev=pe_tr_prev):
                        for h in range(NH):
                            v.wait_ge(sem["SE"], se0 + h + 1)
                            if h < 2:
                                v.wait_ge(sem["PEb"], peb0)
                            else:
                                v.wait_ge(sem["PEb"], peb0 + h - 1)
                            v.scalar_tensor_tensor(
                                PTmsk[:, h % 2, s0:3, :], PTraw[:, h % 2, s0:3, :], 1.0,
                                maskT[:, s0:3, :], ALU.mult, ALU.mult).then_inc(sem["VEa"], 1)
                            v.wait_ge(sem["PEb"], peb0 + h + 1)
                            if pe_tr_prev is not None:
                                v.wait_ge(sem["PEa"], pe_tr_prev + NH + h // 2 + 1)
                            v.reciprocal(rinv[:, h % 2:h % 2 + 1],
                                         psO[h % 2][:, 64:65]).then_inc(sem["VEb"], 1)
                            v.wait_ge(sem["VEb"], veb0 + 2 * h + 1)
                            v.tensor_scalar_mul(att[:, h * 64:(h + 1) * 64],
                                                psO[h % 2][:, 0:64],
                                                rinv[:, h % 2:h % 2 + 1]).then_inc(sem["VEb"], 1)
                        for ob in range(8):
                            v.wait_ge(sem["PEa"], pea0 + NH + ob + 1)
                            v.tensor_copy(afm[:, ob, qt * 128:(qt + 1) * 128],
                                          psTR[ob % 2][:, :]).then_inc(sem["VEa"], 1)
                    pe_tr_prev = pea0

                ve_attn_done = pb.cnt["VEa"]

                pe_o0 = pb.reserve("PE", 8)
                ve_o0 = pb.reserve("VE", 8)

                @block.tensor
                def _(tn, pe_o0=pe_o0, ve_o0=ve_o0):
                    tn.wait_ge(sem["VEa"], ve_attn_done)
                    tn.wait_ge(sem["VE"], ve_v_done)
                    for ob in range(8):
                        if ob >= 2:
                            tn.wait_ge(sem["VE"], ve_o0 + ob - 1)
                        for cb in range(8):
                            mi = tn.matmul(psE[ob % 2][:, 0:SLAB],
                                           WoTs[:, cb, ob * 128:(ob + 1) * 128],
                                           afm[:, cb, :],
                                           start=(cb == 0), stop=(cb == 7))
                        mi.then_inc(sem["PE"], 1)

                @block.vector
                def _(v, pe_o0=pe_o0, st_x1=st_x1):
                    if st_x1:
                        v.wait_ge(sem["ST"], st_x1)
                    for ob in range(8):
                        v.wait_ge(sem["PE"], pe_o0 + ob + 1)
                        v.tensor_add(xo[:, ob, :], XS[:, ob, :],
                                     psE[ob % 2][:, 0:SLAB]).then_inc(sem["VE"], 1)
                ve_xo_resid = ve_o0 + 8

                ve_ln = emit_ln(pb, ln_ctx, xo, 8, SLAB, pre=[("VE", ve_xo_resid)])
                ve_ln_prev = ve_ln

                @block.sync
                def _(sy, t0=t0, s=s, ve_ln=ve_ln):
                    sy.wait_ge(sem["ST"], pb.cnt["ST"])
                    sy.wait_ge(sem["VE"], ve_ln)
                    sy.dma_start(out=hbm[xout_name][s, :, t0:t0 + SLAB]
                                 .rearrange("(c p) n -> p c n", p=128),
                                 in_=xo[:, :, :]).then_inc(sem["ST"], 16)
                st_x1 = pb.bump("ST", 16)
                ve_rope_k_prev = ve_rope["k"]
                pe_v_last = pe_v0 + 2 * QTS

        @block.sync
        def _(sy):
            sy.wait_ge(sem["ST"], pb.cnt["ST"])





B_DEBUG = {}


def emit_phaseB(pb, cfg, hbm, xin_name, xout_name):
    nc, block, sem = pb.nc, pb.block, pb.sem
    dbg_no_top2 = B_DEBUG.get("no_top2", False)
    dbg_nex = B_DEBUG.get("nex", None)
    S, T, TTB = cfg.S, cfg.T, cfg.TTB
    THN = TTB // 512
    TPS = T // TTB
    NT = S * TPS
    NEX = NE + 1
    start = pb.snap()

    with ExitStack() as st:
        x1b = sb(st, nc, "bx1b", [128, 8, TTB], BF16)
        acc = sb(st, nc, "bacc", [128, 8, TTB], F32)
        Wg = sb(st, nc, "bWg", [128, 8, I_FF], BF16)
        Wu = sb(st, nc, "bWu", [128, 8, I_FF], BF16)
        Wd = sb(st, nc, "bWd", [128, 16, D], BF16)
        hbf = sb(st, nc, "bhbf", [128, 16, TTB], BF16)
        sg = sb(st, nc, "bsg", [128, 2, 512], F32)
        uf = sb(st, nc, "buf", [128, 2, 512], F32)
        htmp = sb(st, nc, "bhtmp", [128, 2, 512], F32)
        bcw = sb(st, nc, "bbcw", [128, TTB], F32)
        probs = sb(st, nc, "bprobs", [8, TTB], F32)
        ptm = sb(st, nc, "bptm", [128, 8, 8], F32)
        wtm = sb(st, nc, "bwtm", [128, 8, 8], F32)
        w8 = sb(st, nc, "bw8", [8, TTB], F32)
        mrow = sb(st, nc, "bmrow", [128, 4], F32)
        g1 = sb(st, nc, "bg1", [128, 8], F32)
        pm = sb(st, nc, "bpm", [128, 8], F32)

        onec = sb(st, nc, "bonec", [128, 1], F32)
        oner = sb(st, nc, "boner", [1, 128], F32)
        epsln = sb(st, nc, "bepsln", [1, 1], F32)
        identf = sb(st, nc, "bidentf", [128, 128], F32)
        sel8s = sb(st, nc, "bsel8", [8, 8, 128], F32)
        rbias = sb(st, nc, "brbias", [8, 1], F32)
        WsR = sb(st, nc, "bWsR", [128, 8, 8], F32)
        psG = [ps(st, nc, f"bpsG{i}", [128, 512]) for i in range(2)]
        psU = [ps(st, nc, f"bpsU{i}", [128, 512]) for i in range(2)]
        psD = [ps(st, nc, f"bpsD{i}", [128, 512]) for i in range(2)]
        psR = [ps(st, nc, f"bpsR{i}", [8, 512]) for i in range(THN)]

        ln_ctx = dict(onec=onec, oner=oner, psA=psG[0], psB=psU[0],
                      t1=sg[:, 0, :], t2=sg[:, 1, :], r0=htmp[0:1, 0, :],
                      r1=htmp[0:1, 1, :], r2=uf[0:1, 0, :], eps_ap=epsln)

        def wsrc(e):
            if e == 0:
                return ("WsgT", "WsuT", "WsdT", None)
            return ("WgT", "WuT", "WdT", e - 1)

        def load_expert(g, e, parts=("g", "u", "d")):
            gn, un, dn, ei = wsrc(e)
            for pnm, dst in (("g", Wg), ("u", Wu), ("d", Wd)):
                if pnm not in parts:
                    continue
                nm = {"g": gn, "u": un, "d": dn}[pnm]
                src = hbm[nm] if ei is None else hbm[nm][ei]
                g.dma_start(out=dst[:], in_=r128(src)).then_inc(sem["LG"], 16)

        @block.sync
        def _(sy):
            pb.barrier(sy, start)
            sy.wait_ge(sem["LD"], pb.cnt["LD"])
            sy.dma_start(out=onec[:], in_=hbm["ones_col"]).then_inc(sem["LD"], 16)
            sy.dma_start(out=oner[:], in_=hbm["ones_row"]).then_inc(sem["LD"], 16)
            sy.dma_start(out=epsln[:], in_=hbm["eps_ln"]).then_inc(sem["LD"], 16)
            sy.dma_start(out=identf[:], in_=hbm["ident"]).then_inc(sem["LD"], 16)
            sy.dma_start(out=sel8s[:], in_=hbm["sel8"]).then_inc(sem["LD"], 16)
            sy.dma_start(out=rbias[:], in_=hbm["router_b"]).then_inc(sem["LD"], 16)
        ld_consts = pb.bump("LD", 16 * 6)

        @block.gpsimd
        def _(g):
            pb.barrier(g, start)
            g.wait_ge(sem["LG"], pb.cnt["LG"])
            g.dma_start(out=WsR[:], in_=r128(hbm["routerT"])).then_inc(sem["LG"], 16)
            load_expert(g, 0)
        lg_exp = pb.bump("LG", 16 * 4)
        lg_gu = {(0, 0): lg_exp}
        lg_d = {(0, 0): lg_exp}

        st_x2 = 0
        ve_ln_last = 0
        se_w8_prev = 0
        prev_ve_acc = 0
        psgu_free_ve = 0

        for tile in range(NT):
            s = tile // TPS
            t0 = (tile % TPS) * TTB

            @block.sync
            def _(sy, s=s, t0=t0, st_x2=st_x2, ve_ln_last=ve_ln_last):
                sy.wait_ge(sem["LD"], pb.cnt["LD"])
                if st_x2:
                    sy.wait_ge(sem["ST"], st_x2)
                if ve_ln_last:
                    sy.wait_ge(sem["VE"], ve_ln_last)
                sy.dma_start(out=acc[:], in_=hbm[xin_name][s, :, t0:t0 + TTB]
                             .rearrange("(c p) n -> p c n", p=128)).then_inc(sem["LD"], 16)
            ld_acc = pb.bump("LD", 16)

            @block.gpsimd
            def _(g, s=s, t0=t0):
                g.wait_ge(sem["LG"], pb.cnt["LG"])
                g.wait_ge(sem["PE"], pb.cnt["PE"])
                g.wait_ge(sem["PEb"], pb.cnt["PEb"])
                g.dma_start(out=x1b[:], in_=hbm[xin_name][s, :, t0:t0 + TTB]
                            .rearrange("(c p) n -> p c n", p=128)).then_inc(sem["LG"], 16)
            lg_x1b = pb.bump("LG", 16)

            pe_r0 = pb.reserve("PE", 1)

            @block.tensor
            def _(tn, lg_x1b=lg_x1b, se_w8_prev=se_w8_prev, ld_acc=ld_acc):
                tn.wait_ge(sem["LG"], lg_x1b)
                tn.wait_ge(sem["LD"], ld_acc)
                if se_w8_prev:
                    tn.wait_ge(sem["SE"], se_w8_prev)
                if tile == 0:
                    tn.wait_ge(sem["LG"], lg_exp)
                    tn.wait_ge(sem["LD"], ld_consts)
                for th in range(THN):
                    for cb in range(8):
                        mi = tn.matmul(psR[th][:, :],
                                       WsR[:, cb, :],
                                       acc[:, cb, th * 512:(th + 1) * 512],
                                       start=(cb == 0), stop=(cb == 7))
                mi.then_inc(sem["PE"], 1)

            @block.scalar
            def _(s_, pe_r0=pe_r0):
                s_.wait_ge(sem["PE"], pe_r0 + 1)
                for th in range(THN):
                    mi = s_.activation(probs[:, th * 512:(th + 1) * 512], psR[th][:, :],
                                       AF.Sigmoid, bias=rbias[:, 0:1])
                mi.then_inc(sem["SE"], 1)
            se_probs = pb.bump("SE")

            nch = 0 if dbg_no_top2 else TTB // 128
            chunk_vea = {}
            for c in range(nch):
                pe_t = pb.reserve("PEa", 1)

                @block.tensor
                def _(tn, c=c, pe_t=pe_t, prev_ve_acc=prev_ve_acc):
                    tn.wait_ge(sem["SE"], se_probs)
                    if c < 2:
                        if prev_ve_acc:
                            tn.wait_ge(sem["VE"], prev_ve_acc)
                    else:
                        tn.wait_ge(sem["VEa"], chunk_vea[c - 2] + 1)
                    tn.transpose(psD[c % 2][:, 0:8], probs[:, c * 128:(c + 1) * 128],
                                 identf[0:8, 0:8]).then_inc(sem["PEa"], 1)

                vea0 = pb.reserve("VEa", 8)
                chunk_vea[c] = vea0

                @block.vector
                def _(v, c=c, pe_t=pe_t, vea0=vea0):
                    v.wait_ge(sem["PEa"], pe_t + 1)
                    v.tensor_copy(ptm[:, c, :], psD[c % 2][:, 0:8]).then_inc(sem["VEa"], 1)
                    v.wait_ge(sem["VEa"], vea0 + 1)
                    v.reduce_max(mrow[:, 0:1], ptm[:, c, :], axis=AX.X).then_inc(sem["VEa"], 1)
                    v.wait_ge(sem["VEa"], vea0 + 2)
                    v.tensor_single_scalar(g1[:, :], ptm[:, c, :], mrow[:, 0:1],
                                           ALU.is_ge).then_inc(sem["VEa"], 1)
                    v.wait_ge(sem["VEa"], vea0 + 3)
                    v.scalar_tensor_tensor(pm[:, :], g1[:, :], -1e30, ptm[:, c, :],
                                           ALU.mult, ALU.add).then_inc(sem["VEa"], 1)
                    v.wait_ge(sem["VEa"], vea0 + 4)
                    v.reduce_max(mrow[:, 1:2], pm[:, :], axis=AX.X).then_inc(sem["VEa"], 1)
                    v.wait_ge(sem["VEa"], vea0 + 5)
                    v.tensor_add(mrow[:, 2:3], mrow[:, 0:1], mrow[:, 1:2])
                    v.tensor_single_scalar(g1[:, :], ptm[:, c, :], mrow[:, 1:2],
                                           ALU.is_ge).then_inc(sem["VEa"], 1)
                    v.wait_ge(sem["VEa"], vea0 + 6)
                    v.reciprocal(mrow[:, 3:4], mrow[:, 2:3])
                    v.tensor_mul(pm[:, :], ptm[:, c, :], g1[:, :]).then_inc(sem["VEa"], 1)
                    v.wait_ge(sem["VEa"], vea0 + 7)
                    v.tensor_single_scalar(wtm[:, c, :], pm[:, :], mrow[:, 3:4],
                                           ALU.mult).then_inc(sem["VEa"], 1)

            if not dbg_no_top2:
                pe_tb = pb.reserve("PEa", 1)

                @block.tensor
                def _(tn, pe_tb=pe_tb):
                    tn.wait_ge(sem["VEa"], pb.cnt["VEa"])
                    for c in range(nch):
                        mi = tn.transpose(psR[c // 4][:, (c % 4) * 128:(c % 4 + 1) * 128],
                                          wtm[:, c, :], identf[:, :])
                    mi.then_inc(sem["PEa"], 1)

                @block.scalar
                def _(s_, pe_tb=pe_tb):
                    s_.wait_ge(sem["PEa"], pe_tb + 1)
                    for th in range(THN):
                        mi = s_.copy(w8[:, th * 512:(th + 1) * 512], psR[th][:, :])
                    mi.then_inc(sem["SE"], 1)
                se_w8 = pb.bump("SE")
                se_w8_prev = se_w8
            else:
                @block.vector
                def _(v):
                    v.memset(w8[:, :], 0.1).then_inc(sem["VE"], 1)
                pb.bump("VE")
                se_w8 = 0

            NEXL = NEX if dbg_nex is None else dbg_nex
            for e in range(NEXL):
                if e > 0:
                    pe_bc = pb.reserve("PEb", THN)
                    ve_bc0 = pb.reserve("VE", THN)

                    @block.tensor
                    def _(tn, e=e, pe_bc=pe_bc, ve_bc0=ve_bc0,
                          prev_ve_acc=prev_ve_acc):
                        if se_w8:
                            tn.wait_ge(sem["SE"], se_w8)
                        if prev_ve_acc:
                            tn.wait_ge(sem["VE"], prev_ve_acc)
                        for th in range(THN):
                            if th > 0:
                                tn.wait_ge(sem["VE"], ve_bc0 + th)
                            mi = tn.matmul(psD[1][:, 0:512],
                                           sel8s[:, e - 1, :],
                                           w8[:, th * 512:(th + 1) * 512],
                                           start=True, stop=True)
                            mi.then_inc(sem["PEb"], 1)

                    @block.vector
                    def _(v, pe_bc=pe_bc, ve_bc0=ve_bc0):
                        for th in range(THN):
                            v.wait_ge(sem["PEb"], pe_bc + th + 1)
                            v.tensor_copy(bcw[:, th * 512:(th + 1) * 512],
                                          psD[1][:, :]).then_inc(sem["VE"], 1)
                    ve_bcw = ve_bc0 + THN
                else:
                    ve_bcw = 0

                NIDX = 16 * THN
                pe_g0 = pb.reserve("PE", NIDX)
                pe_u0 = pb.reserve("PE", NIDX)
                se_h0 = pb.reserve("SE", 2 * NIDX)
                ve_h0 = pb.reserve("VE", (2 if e > 0 else 1) * NIDX)
                nvh = 2 if e > 0 else 1

                @block.tensor
                def _(tn, e=e, pe_g0=pe_g0, pe_u0=pe_u0, ve_h0=ve_h0, nvh=nvh,
                      ve_bcw=ve_bcw, psgu_free_ve=psgu_free_ve):
                    tn.wait_ge(sem["LG"], lg_gu[(tile, e)])
                    if psgu_free_ve:
                        tn.wait_ge(sem["VE"], psgu_free_ve)
                    for pbi in range(16):
                        for th in range(THN):
                            idx = pbi * THN + th
                            if idx >= 2:
                                tn.wait_ge(sem["VE"], ve_h0 + nvh * (idx - 1))
                            for cb in range(8):
                                mg = tn.matmul(psG[idx % 2][:, :],
                                               Wg[:, cb, pbi * 128:(pbi + 1) * 128],
                                               x1b[:, cb, th * 512:(th + 1) * 512],
                                               start=(cb == 0), stop=(cb == 7))
                            mg.then_inc(sem["PE"], 1)
                            for cb in range(8):
                                mu = tn.matmul(psU[idx % 2][:, :],
                                               Wu[:, cb, pbi * 128:(pbi + 1) * 128],
                                               x1b[:, cb, th * 512:(th + 1) * 512],
                                               start=(cb == 0), stop=(cb == 7))
                            mu.then_inc(sem["PE"], 1)

                @block.scalar
                def _(s_, pe_g0=pe_g0, pe_u0=pe_u0, se_h0=se_h0, ve_h0=ve_h0,
                      nvh=nvh):
                    for idx in range(NIDX):
                        s_.wait_ge(sem["PE"], pe_g0 + 2 * idx + 1)
                        if idx >= 2:
                            s_.wait_ge(sem["VE"], ve_h0 + nvh * (idx - 1))
                        s_.activation(sg[:, idx % 2, :], psG[idx % 2][:, :],
                                      AF.Silu).then_inc(sem["SE"], 1)
                        s_.wait_ge(sem["PE"], pe_g0 + 2 * idx + 2)
                        s_.copy(uf[:, idx % 2, :], psU[idx % 2][:, :]).then_inc(sem["SE"], 1)

                @block.vector
                def _(v, e=e, se_h0=se_h0, ve_h0=ve_h0, nvh=nvh, ve_bcw=ve_bcw):
                    if e > 0:
                        v.wait_ge(sem["VE"], ve_bcw)
                    for pbi in range(16):
                        for th in range(THN):
                            idx = pbi * THN + th
                            v.wait_ge(sem["SE"], se_h0 + 2 * idx + 2)
                            if e == 0:
                                v.tensor_mul(hbf[:, pbi, th * 512:(th + 1) * 512],
                                             sg[:, idx % 2, :],
                                             uf[:, idx % 2, :]).then_inc(sem["VE"], 1)
                            else:
                                v.tensor_mul(htmp[:, idx % 2, :],
                                             sg[:, idx % 2, :],
                                             uf[:, idx % 2, :]).then_inc(sem["VE"], 1)
                                v.wait_ge(sem["VE"], ve_h0 + 2 * idx + 1)
                                v.tensor_mul(hbf[:, pbi, th * 512:(th + 1) * 512],
                                             htmp[:, idx % 2, :],
                                             bcw[:, th * 512:(th + 1) * 512]).then_inc(sem["VE"], 1)
                ve_h_done = ve_h0 + nvh * NIDX
                psgu_free_ve = ve_h_done

                nxt = e + 1
                ntile = tile
                if nxt == NEXL:
                    nxt = 0
                    ntile = tile + 1
                do_pf = ntile < NT

                if do_pf:
                    @block.gpsimd
                    def _(g, nxt=nxt, pe_g0=pe_g0):
                        g.wait_ge(sem["LG"], pb.cnt["LG"])
                        g.wait_ge(sem["PE"], pe_g0 + 2 * NIDX)
                        load_expert(g, nxt, parts=("g", "u"))
                    lg_gu[(ntile, nxt)] = pb.bump("LG", 16 * 2)

                pe_d0 = pb.reserve("PEb", 8 * THN)
                ve_d0 = pb.reserve("VE", 8 * THN)

                @block.tensor
                def _(tn, e=e, pe_d0=pe_d0, ve_d0=ve_d0, ve_h_done=ve_h_done):
                    tn.wait_ge(sem["VE"], ve_h_done)
                    tn.wait_ge(sem["LG"], lg_d[(tile, e)])
                    for ob in range(8):
                        for th in range(THN):
                            idx = ob * THN + th
                            if idx >= 2:
                                tn.wait_ge(sem["VE"], ve_d0 + idx - 1)
                            for cb in range(16):
                                md = tn.matmul(psD[idx % 2][:, :],
                                               Wd[:, cb, ob * 128:(ob + 1) * 128],
                                               hbf[:, cb, th * 512:(th + 1) * 512],
                                               start=(cb == 0), stop=(cb == 15))
                            md.then_inc(sem["PEb"], 1)

                @block.vector
                def _(v, pe_d0=pe_d0, ve_d0=ve_d0, ld_acc=ld_acc):
                    if e == 0:
                        v.wait_ge(sem["LD"], ld_acc)
                    for ob in range(8):
                        for th in range(THN):
                            idx = ob * THN + th
                            v.wait_ge(sem["PEb"], pe_d0 + idx + 1)
                            v.tensor_add(acc[:, ob, th * 512:(th + 1) * 512],
                                         acc[:, ob, th * 512:(th + 1) * 512],
                                         psD[idx % 2][:, :]).then_inc(sem["VE"], 1)
                ve_acc = ve_d0 + 8 * THN
                prev_ve_acc = ve_acc

                if do_pf:
                    @block.gpsimd
                    def _(g, nxt=nxt):
                        g.wait_ge(sem["LG"], pb.cnt["LG"])
                        g.wait_ge(sem["PEb"], pb.cnt["PEb"])
                        load_expert(g, nxt, parts=("d",))
                    lg_d[(ntile, nxt)] = pb.bump("LG", 16)

            ve_lns = []
            ve_pre = ve_acc
            for th in range(THN):
                xslice = acc[:, :, th * 512:(th + 1) * 512]
                ve_l = emit_ln(pb, ln_ctx, xslice, 8, 512,
                               pre=[("VE", ve_pre), ("PEb", pb.cnt["PEb"])])
                ve_lns.append(ve_l)
                ve_pre = ve_l
            ve_ln_last = ve_lns[-1]
            psgu_free_ve = ve_ln_last

            @block.sync
            def _(sy, s=s, t0=t0, ve_lns=ve_lns):
                sy.wait_ge(sem["ST"], pb.cnt["ST"])
                for v in ve_lns:
                    sy.wait_ge(sem["VE"], v)
                sy.dma_start(out=hbm[xout_name][s, :, t0:t0 + TTB]
                             .rearrange("(c p) n -> p c n", p=128),
                             in_=acc[:, :, :]).then_inc(sem["ST"], 16)
            st_x2 = pb.bump("ST", 16)

        @block.sync
        def _(sy):
            sy.wait_ge(sem["ST"], pb.cnt["ST"])


PHASE_EMIT["B"] = emit_phaseB





def emit_phaseC(pb, cfg, hbm, xin_name, xout_name):
    nc, block, sem = pb.nc, pb.block, pb.sem
    S, T, TC = cfg.S, cfg.T, cfg.TC
    NCH = T // TC
    start = pb.snap()

    with ExitStack() as st:
        XC = sb(st, nc, "cXC", [128, 8, TC], F32)
        hb = sb(st, nc, "chb", [128, 8, TC], BF16)
        WipU = sb(st, nc, "cWipU", [128, 8, DI], BF16)
        Wsh = sb(st, nc, "cWsh", [128, 16384], BF16)
        u_sb = sb(st, nc, "cu", [128, 16, TC + 3], F32)
        utail = sb(st, nc, "cutail", [128, 16, 3], F32)
        uc = sb(st, nc, "cuc", [128, 16, TC], F32)
        dt = sb(st, nc, "cdt", [128, 16, TC], F32)
        zb = sb(st, nc, "cz", [128, 16, TC], F32)
        yb = sb(st, nc, "cy", [128, 16, TC], F32)
        dBu = sb(st, nc, "cdBu", [128, 16, TC], F32)
        ygb = sb(st, nc, "cyg", [128, 16, TC], BF16)
        bcB = sb(st, nc, "cbcB", [128, 1, TC], F32)
        bcC = sb(st, nc, "cbcC", [128, 1, TC], F32)
        carry = sb(st, nc, "ccarry", [128, 16, NSTATE], F32)
        xdbl = sb(st, nc, "cxdbl", [128, TC], F32)
        dtin = sb(st, nc, "cdtin", [64, TC], BF16)
        spx = sb(st, nc, "cspx", [128, 2, TC], F32)
        dA2 = spx
        spa = ct1 = sb(st, nc, "cct1", [128, TC], F32)
        ct2 = sb(st, nc, "cct2", [128, TC], F32)
        spr = ct2
        WxpS = sb(st, nc, "cWxp", [128, 16, 96], BF16)
        WdtS = sb(st, nc, "cWdt", [64, 16, 128], BF16)
        ecol = sb(st, nc, "cecol", [32, 32], F32)
        bc_sb = sb(st, nc, "cbcsb", [32, TC], F32)

        Acol = sb(st, nc, "cAcol", [128, 16, NSTATE], F32)
        AlogS = Acol
        normw = sb(st, nc, "cnormw", [128, 8], F32)
        convw = sb(st, nc, "cconvw", [128, 16, KCONV], F32)
        convb = sb(st, nc, "cconvb", [128, 16], F32)
        dtb = sb(st, nc, "cdtb", [128, 16], F32)
        Dcol = sb(st, nc, "cDcol", [128, 16], F32)
        t1, t2 = ct1, ct2
        rowbuf = sb(st, nc, "crowbuf", [1, 7, TC], F32)
        rows = rowbuf[:, 0:3, :]
        rowB = rowbuf[:, 3:5, :]
        rowC = rowbuf[:, 5:7, :]
        onec = sb(st, nc, "conec", [128, 1], F32)
        oner = sb(st, nc, "coner", [1, 128], F32)
        epsln = sb(st, nc, "cepsln", [1, 1], F32)
        epsrms = sb(st, nc, "cepsrms", [1, 1], F32)
        psI = [ps(st, nc, f"cpsI{i}", [128, 512]) for i in range(2)]
        psX = ps(st, nc, "cpsX", [128, 512])
        psB2 = [ps(st, nc, f"cpsB2{i}", [128, 512]) for i in range(2)]
        psC2 = [ps(st, nc, f"cpsC2{i}", [128, 512]) for i in range(2)]

        ln_ctx = dict(onec=onec, oner=oner, psA=psI[0], psB=psI[1],
                      t1=t1, t2=t2, r0=rows[:, 0, :], r1=rows[:, 1, :],
                      r2=rows[:, 2, :], eps_ap=epsln)

        @block.sync
        def _(sy):
            pb.barrier(sy, start)
            sy.wait_ge(sem["LD"], pb.cnt["LD"])
            for dst, nm in ((onec, "ones_col"), (oner, "ones_row"),
                            (epsln, "eps_ln"), (epsrms, "eps_rms"),
                            (ecol, "esel_cols"), (AlogS, "Alog_fm"),
                            (normw, "normw_fm"), (convw, "convw_fm"),
                            (convb, "convb_fm"), (dtb, "dtb_fm"),
                            (Dcol, "D_fm")):
                sy.dma_start(out=dst[:], in_=hbm[nm]).then_inc(sem["LD"], 16)
        ld_consts = pb.bump("LD", 16 * 11)

        @block.gpsimd
        def _(g):
            pb.barrier(g, start)
            g.wait_ge(sem["LG"], pb.cnt["LG"])
            g.dma_start(out=WipU[:], in_=hbm["WipT"][:, 0:DI]
                        .rearrange("(c p) n -> p c n", p=128)).then_inc(sem["LG"], 16)
            g.dma_start(out=WxpS[:], in_=hbm["WxpT"]
                        .rearrange("(c p) n -> p c n", p=128)).then_inc(sem["LG"], 16)
            g.dma_start(out=WdtS[:], in_=hbm["WdtT"]
                        .rearrange("c (b n) -> c b n", n=128)).then_inc(sem["LG"], 16)
        lg_init = pb.bump("LG", 16 * 3)

        se_A0 = pb.reserve("SE", 1)
        ve_A0 = pb.reserve("VE", 1)

        @block.scalar
        def _(s_):
            s_.wait_ge(sem["LD"], ld_consts)
            s_.activation(Acol[:, :, :], Acol[:, :, :], AF.Exp).then_inc(sem["SE"], 1)

        @block.vector
        def _(v):
            v.wait_ge(sem["SE"], se_A0 + 1)
            v.tensor_scalar_mul(Acol[:, :, :], Acol[:, :, :], -1.0).then_inc(sem["VE"], 1)
        ve_Acol = ve_A0 + 1

        st_x3 = 0
        ve_ln_prev = 0
        pe_op_prev = 0
        se_xdbl_prev = 0
        ve_utail_prev = 0
        prev_scan_done = 0

        for si in range(S):
            for ci in range(NCH):
                t0 = ci * TC
                first = (si == 0 and ci == 0)

                @block.sync
                def _(sy, si=si, t0=t0, st_x3=st_x3):
                    sy.wait_ge(sem["LD"], pb.cnt["LD"])
                    if st_x3:
                        sy.wait_ge(sem["ST"], st_x3)
                    if ve_ln_prev:
                        sy.wait_ge(sem["VE"], ve_ln_prev)
                    sy.dma_start(out=XC[:], in_=hbm[xin_name][si, :, t0:t0 + TC]
                                 .rearrange("(c p) n -> p c n", p=128)).then_inc(sem["LD"], 16)
                ld_xc = pb.bump("LD", 16)

                @block.gpsimd
                def _(g, pe_op_prev=pe_op_prev):
                    g.wait_ge(sem["LG"], pb.cnt["LG"])
                    if pe_op_prev:
                        g.wait_ge(sem["PE"], pe_op_prev)
                    g.dma_start(out=Wsh[:, :].rearrange("p (c n) -> p c n", c=8),
                                in_=hbm["WipT"][:, DI:2 * DI]
                                .rearrange("(c p) n -> p c n", p=128)).then_inc(sem["LG"], 16)
                lg_z = pb.bump("LG", 16)

                se_sq0 = pb.reserve("SE", 8)
                pe_sq0 = pb.reserve("PE", 8)

                @block.scalar
                def _(s_, ld_xc=ld_xc, se_sq0=se_sq0, pe_sq0=pe_sq0, first=first,
                      pe_tot=None):
                    s_.wait_ge(sem["LD"], ld_xc)
                    if first:
                        s_.wait_ge(sem["LD"], ld_consts)
                    if ve_ln_prev:
                        s_.wait_ge(sem["VE"], ve_ln_prev)
                    s_.wait_ge(sem["PE"], pe_sq0)
                    for cb in range(8):
                        if cb >= 2:
                            s_.wait_ge(sem["PE"], pe_sq0 + cb - 1)
                        s_.activation((t1 if cb % 2 == 0 else t2)[:, 0:TC],
                                      XC[:, cb, :], AF.Square).then_inc(sem["SE"], 1)

                @block.tensor
                def _(tn, se_sq0=se_sq0):
                    if ve_ln_prev:
                        tn.wait_ge(sem["VE"], ve_ln_prev)
                    for cb in range(8):
                        tn.wait_ge(sem["SE"], se_sq0 + cb + 1)
                        tn.matmul(psI[0][0:1, 0:TC], onec[:, 0:1],
                                  (t1 if cb % 2 == 0 else t2)[:, 0:TC],
                                  start=(cb == 0), stop=(cb == 7)).then_inc(sem["PE"], 1)

                se_r0 = pb.reserve("SE", 2)

                @block.scalar
                def _(s_, pe_sq0=pe_sq0, se_r0=se_r0):
                    s_.wait_ge(sem["PE"], pe_sq0 + 8)
                    s_.activation(rows[:, 0, 0:TC], psI[0][0:1, 0:TC], AF.Ln,
                                  scale=1.0 / D, bias=epsrms[:, :]).then_inc(sem["SE"], 1)
                    s_.wait_ge(sem["SE"], se_r0 + 1)
                    s_.activation(rows[:, 1, 0:TC], rows[:, 0, 0:TC], AF.Exp,
                                  scale=-0.5).then_inc(sem["SE"], 1)

                pe_bc0 = pb.reserve("PE", 1)

                @block.tensor
                def _(tn, se_r0=se_r0):
                    tn.wait_ge(sem["SE"], se_r0 + 2)
                    tn.matmul(psB2[0][:, 0:TC], oner[0:1, :], rows[:, 1, 0:TC],
                              start=True, stop=True).then_inc(sem["PE"], 1)

                ve_h0 = pb.reserve("VE", 8)

                @block.vector
                def _(v, pe_bc0=pe_bc0, first=first):
                    if first:
                        v.wait_ge(sem["VE"], ve_Acol)
                    v.wait_ge(sem["PE"], pe_bc0 + 1)
                    for cb in range(8):
                        v.scalar_tensor_tensor(hb[:, cb, :], XC[:, cb, :],
                                               normw[:, cb:cb + 1], psB2[0][:, 0:TC],
                                               ALU.mult, ALU.mult).then_inc(sem["VE"], 1)
                ve_hb = ve_h0 + 8

                if ci == 0:
                    ve_c0 = pb.reserve("VE", 1)

                    @block.vector
                    def _(v):
                        v.memset(carry[:, :, :], 0.0)
                        v.memset(utail[:, :, :], 0.0).then_inc(sem["VE"], 1)
                    ve_utail_prev = ve_c0 + 1

                pe_u0 = pb.reserve("PE", 16)
                se_u0 = pb.reserve("SE", 16)

                @block.tensor
                def _(tn, pe_u0=pe_u0, se_u0=se_u0, first=first):
                    tn.wait_ge(sem["VE"], ve_hb)
                    if first:
                        tn.wait_ge(sem["LG"], lg_init)
                    for pbi in range(16):
                        if pbi >= 2:
                            tn.wait_ge(sem["SE"], se_u0 + pbi - 1)
                        for cb in range(8):
                            mi = tn.matmul(psI[pbi % 2][:, 0:TC],
                                           WipU[:, cb, pbi * 128:(pbi + 1) * 128],
                                           hb[:, cb, :],
                                           start=(cb == 0), stop=(cb == 7))
                        mi.then_inc(sem["PE"], 1)

                @block.scalar
                def _(s_, pe_u0=pe_u0, ve_utail_prev=ve_utail_prev,
                      vlp=ve_ln_prev):
                    s_.wait_ge(sem["VE"], ve_utail_prev)
                    if vlp:
                        s_.wait_ge(sem["VE"], vlp)
                    for pbi in range(16):
                        s_.wait_ge(sem["PE"], pe_u0 + pbi + 1)
                        s_.copy(u_sb[:, pbi, 3:3 + TC],
                                psI[pbi % 2][:, 0:TC]).then_inc(sem["SE"], 1)

                pe_z0 = pb.reserve("PE", 16)
                se_z0 = pb.reserve("SE", 16)

                @block.tensor
                def _(tn, pe_z0=pe_z0, se_z0=se_z0, se_u0=se_u0, lg_z=lg_z):
                    tn.wait_ge(sem["LG"], lg_z)
                    tn.wait_ge(sem["SE"], se_u0 + 16)
                    for pbi in range(16):
                        if pbi >= 2:
                            tn.wait_ge(sem["SE"], se_z0 + pbi - 1)
                        for cb in range(8):
                            mi = tn.matmul(psI[pbi % 2][:, 0:TC],
                                           Wsh[:, cb * 2048 + pbi * 128:
                                               cb * 2048 + (pbi + 1) * 128],
                                           hb[:, cb, :],
                                           start=(cb == 0), stop=(cb == 7))
                        mi.then_inc(sem["PE"], 1)
                pe_z_done = pe_z0 + 16

                @block.scalar
                def _(s_, pe_z0=pe_z0, vlp=ve_ln_prev):
                    if vlp:
                        s_.wait_ge(sem["VE"], vlp)
                    for pbi in range(16):
                        s_.wait_ge(sem["PE"], pe_z0 + pbi + 1)
                        s_.copy(zb[:, pbi, :],
                                psI[pbi % 2][:, 0:TC]).then_inc(sem["SE"], 1)

                @block.gpsimd
                def _(g, pe_z_done=pe_z_done):
                    g.wait_ge(sem["LG"], pb.cnt["LG"])
                    g.wait_ge(sem["PE"], pe_z_done)
                    g.dma_start(out=Wsh[:, :].rearrange("p (c n) -> p c n", c=16),
                                in_=hbm["WopT"]
                                .rearrange("(c p) n -> p c n", p=128)).then_inc(sem["LG"], 16)
                lg_op = pb.bump("LG", 16)

                ve_tl = pb.reserve("VE", 1)

                @block.vector
                def _(v, se_u0=se_u0):
                    v.wait_ge(sem["SE"], se_u0 + 16)
                    v.tensor_copy(u_sb[:, :, 0:3], utail[:, :, :]).then_inc(sem["VE"], 1)

                ve_cv0 = pb.reserve("VE", 4 * 16)
                se_cv0 = pb.reserve("SE", 16)

                @block.vector
                def _(v, ve_cv0=ve_cv0, se_cv0=se_cv0, ve_tl=ve_tl, pe_sq0=pe_sq0):
                    v.wait_ge(sem["VE"], ve_tl + 1)
                    v.wait_ge(sem["PE"], pe_sq0 + 8)
                    for pbi in range(16):
                        if pbi >= 1:
                            v.wait_ge(sem["VE"], ve_cv0 + 4 * pbi)
                            v.wait_ge(sem["SE"], se_cv0 + pbi)
                        v.tensor_scalar(ct1[:, :], u_sb[:, pbi, 0:TC],
                                        convw[:, pbi, 0:1], convb[:, pbi:pbi + 1],
                                        ALU.mult, ALU.add).then_inc(sem["VE"], 1)
                        v.wait_ge(sem["VE"], ve_cv0 + 4 * pbi + 1)
                        v.scalar_tensor_tensor(ct2[:, :], u_sb[:, pbi, 1:1 + TC],
                                               convw[:, pbi, 1:2], ct1[:, :],
                                               ALU.mult, ALU.add).then_inc(sem["VE"], 1)
                        v.wait_ge(sem["VE"], ve_cv0 + 4 * pbi + 2)
                        v.scalar_tensor_tensor(ct1[:, :], u_sb[:, pbi, 2:2 + TC],
                                               convw[:, pbi, 2:3], ct2[:, :],
                                               ALU.mult, ALU.add).then_inc(sem["VE"], 1)
                        v.wait_ge(sem["VE"], ve_cv0 + 4 * pbi + 3)
                        v.scalar_tensor_tensor(ct2[:, :], u_sb[:, pbi, 3:3 + TC],
                                               convw[:, pbi, 3:4], ct1[:, :],
                                               ALU.mult, ALU.add).then_inc(sem["VE"], 1)

                @block.scalar
                def _(s_, ve_cv0=ve_cv0):
                    for pbi in range(16):
                        s_.wait_ge(sem["VE"], ve_cv0 + 4 * pbi + 4)
                        s_.activation(uc[:, pbi, :], ct2[:, :],
                                      AF.Silu).then_inc(sem["SE"], 1)
                se_uc = se_cv0 + 16

                ve_ut = pb.reserve("VE", 1)

                @block.vector
                def _(v):
                    v.tensor_copy(utail[:, :, :],
                                  u_sb[:, :, TC:TC + 3]).then_inc(sem["VE"], 1)
                ve_utail_prev = ve_ut + 1

                ve_ucb = pb.reserve("VE", 1)

                @block.vector
                def _(v, se_cv0=se_cv0, pe_op_prev=pe_op_prev):
                    v.wait_ge(sem["SE"], se_cv0 + 16)
                    if pe_op_prev:
                        v.wait_ge(sem["PE"], pe_op_prev)
                    v.tensor_copy(ygb[:, :, :], uc[:, :, :]).then_inc(sem["VE"], 1)

                pe_xd = pb.reserve("PE", 1)

                @block.tensor
                def _(tn, ve_ucb=ve_ucb, se_xdbl_prev=se_xdbl_prev):
                    tn.wait_ge(sem["VE"], ve_ucb + 1)
                    tn.wait_ge(sem["PEb"], pb.cnt["PEb"])
                    if se_xdbl_prev:
                        tn.wait_ge(sem["SE"], se_xdbl_prev)
                    for cb in range(16):
                        tn.matmul(psX[0:64, 0:TC], WxpS[:, cb, 0:64],
                                  ygb[:, cb, :], start=(cb == 0), stop=(cb == 15))
                    for cb in range(16):
                        mi = tn.matmul(psX[0:32, 256:256 + TC], WxpS[:, cb, 64:96],
                                       ygb[:, cb, :], start=(cb == 0), stop=(cb == 15))
                    mi.then_inc(sem["PE"], 1)

                se_xd = pb.reserve("SE", 1)

                @block.scalar
                def _(s_, pe_xd=pe_xd):
                    s_.wait_ge(sem["PE"], pe_xd + 1)
                    s_.copy(xdbl[0:64, :], psX[0:64, 0:TC])
                    s_.copy(bc_sb[:, :], psX[0:32, 256:256 + TC]).then_inc(sem["SE"], 1)
                se_xdbl_prev = se_xd + 1

                ve_dtin = pb.reserve("VE", 1)

                @block.vector
                def _(v, se_xd=se_xd):
                    v.wait_ge(sem["SE"], se_xd + 1)
                    v.tensor_copy(dtin[:, :], xdbl[0:64, :]).then_inc(sem["VE"], 1)

                pe_dt0 = pb.reserve("PE", 16)
                se_sp0 = pb.reserve("SE", 4 * 16)
                ve_sp0 = pb.reserve("VE", 2 * 16)

                @block.tensor
                def _(tn, pe_dt0=pe_dt0, se_sp0=se_sp0, se_z0=se_z0, ve_dtin=ve_dtin):
                    tn.wait_ge(sem["VE"], ve_dtin + 1)
                    tn.wait_ge(sem["SE"], se_z0 + 16)
                    for pbi in range(16):
                        if pbi >= 2:
                            tn.wait_ge(sem["SE"], se_sp0 + 4 * (pbi - 2) + 1)
                        tn.matmul(psI[pbi % 2][:, 0:TC], WdtS[:, pbi, :],
                                  dtin[:, :], start=True, stop=True).then_inc(sem["PE"], 1)

                @block.scalar
                def _(s_, pe_dt0=pe_dt0, se_sp0=se_sp0, ve_sp0=ve_sp0):
                    for pbi in range(16):
                        s_.wait_ge(sem["PE"], pe_dt0 + pbi + 1)
                        if pbi >= 2:
                            s_.wait_ge(sem["VE"], ve_sp0 + 2 * (pbi - 2) + 2)
                        s_.activation(spx[:, pbi % 2, :], psI[pbi % 2][:, 0:TC],
                                      AF.Identity, bias=dtb[:, pbi:pbi + 1]).then_inc(sem["SE"], 1)
                        s_.wait_ge(sem["SE"], se_sp0 + 4 * pbi + 1)
                        if pbi >= 1:
                            s_.wait_ge(sem["VE"], ve_sp0 + 2 * pbi)
                        s_.activation(spa[:, :], spx[:, pbi % 2, :],
                                      AF.Abs).then_inc(sem["SE"], 1)
                        s_.wait_ge(sem["SE"], se_sp0 + 4 * pbi + 2)
                        s_.activation(spa[:, :], spa[:, :], AF.Exp,
                                      scale=-1.0).then_inc(sem["SE"], 1)
                        s_.wait_ge(sem["SE"], se_sp0 + 4 * pbi + 3)
                        s_.activation(spa[:, :], spa[:, :], AF.Ln,
                                      bias=onec[:, 0:1]).then_inc(sem["SE"], 1)

                @block.vector
                def _(v, se_sp0=se_sp0, ve_sp0=ve_sp0):
                    for pbi in range(16):
                        v.wait_ge(sem["SE"], se_sp0 + 4 * pbi + 4)
                        v.tensor_scalar_max(spr[:, :], spx[:, pbi % 2, :],
                                            0.0).then_inc(sem["VE"], 1)
                        v.wait_ge(sem["VE"], ve_sp0 + 2 * pbi + 1)
                        v.tensor_add(dt[:, pbi, :], spr[:, :],
                                     spa[:, :]).then_inc(sem["VE"], 1)
                ve_dt_done = ve_sp0 + 32

                ve_dtu = pb.reserve("VE", 1)

                @block.vector
                def _(v, ve_dt_done=ve_dt_done):
                    v.wait_ge(sem["VE"], ve_dt_done)
                    v.tensor_mul(u_sb[:, :, 0:TC], dt[:, :, :],
                                 uc[:, :, :]).then_inc(sem["VE"], 1)
                ve_dtu_done = ve_dtu + 1

                scan_base = {}
                bc_base = {}
                row_se = {}
                gp_dbu = {}
                prev_gp_dict = None
                prev_gp = 0
                for n in range(NSTATE):
                    pe_b1 = pb.reserve("PEb", 1)

                    @block.tensor
                    def _(tn, n=n, se_xd=se_xd):
                        tn.wait_ge(sem["SE"], se_xd + 1)
                        if n < 2:
                            tn.wait_ge(sem["VE"], ve_hb)
                            if prev_scan_done:
                                tn.wait_ge(sem["VE"], prev_scan_done)
                        else:
                            tn.wait_ge(sem["VEa"], bc_base[n - 2] + 2)
                            tn.wait_ge(sem["SE"], row_se[n - 2] + 1)
                        tn.matmul(psB2[n % 2][0:1, 256:256 + TC], ecol[:, n:n + 1],
                                  bc_sb[:, :], start=True, stop=True)
                        tn.matmul(psC2[n % 2][0:1, 256:256 + TC], ecol[:, 16 + n:17 + n],
                                  bc_sb[:, :], start=True,
                                  stop=True).then_inc(sem["PEb"], 1)

                    se_row = pb.reserve("SE", 1)
                    row_se[n] = se_row

                    @block.scalar
                    def _(s_, n=n, pe_b1=pe_b1, se_row=se_row):
                        s_.wait_ge(sem["PEb"], pe_b1 + 1)
                        if n >= 2:
                            s_.wait_ge(sem["PEb"], pb.cnt["PEb"] - 1)
                        s_.copy(rowB[:, n % 2, :], psB2[n % 2][0:1, 256:256 + TC])
                        s_.copy(rowC[:, n % 2, :],
                                psC2[n % 2][0:1, 256:256 + TC]).then_inc(sem["SE"], 1)

                    pe_b2 = pb.reserve("PEb", 1)

                    @block.tensor
                    def _(tn, n=n, se_row=se_row):
                        tn.wait_ge(sem["SE"], se_row + 1)
                        tn.matmul(psB2[n % 2][:, 0:TC], oner[0:1, :],
                                  rowB[:, n % 2, :], start=True, stop=True)
                        tn.matmul(psC2[n % 2][:, 0:TC], oner[0:1, :],
                                  rowC[:, n % 2, :], start=True,
                                  stop=True).then_inc(sem["PEb"], 1)

                    vea0 = pb.reserve("VEa", 2)
                    bc_base[n] = vea0

                    @block.vector
                    def _(v, n=n, pe_b2=pe_b2, vea0=vea0):
                        v.wait_ge(sem["PEb"], pe_b2 + 1)
                        if n >= 1:
                            v.wait_ge(sem["VE"], prev_scan_done)
                        v.tensor_copy(bcB[:, 0, :], psB2[n % 2][:, 0:TC]).then_inc(sem["VEa"], 1)
                        v.wait_ge(sem["VEa"], vea0 + 1)
                        v.tensor_copy(bcC[:, 0, :], psC2[n % 2][:, 0:TC]).then_inc(sem["VEa"], 1)

                    se_sc0 = pb.reserve("SE", NSTATE)
                    nve = NSTATE + 4
                    ve_s0 = pb.reserve("VE", nve if n > 0 else nve - 1)
                    scan_base[n] = ve_s0
                    prev_sd = prev_scan_done

                    @block.scalar
                    def _(s_, n=n, se_sc0=se_sc0, ve_s0=ve_s0):
                        if n == 0:
                            s_.wait_ge(sem["VE"], ve_dt_done)
                        for pbi in range(NSTATE):
                            if pbi < 2:
                                if n > 0:
                                    s_.wait_ge(sem["VE"], prev_sd)
                            else:
                                s_.wait_ge(sem["VE"], ve_s0 + pbi)
                            s_.activation(dA2[:, pbi % 2, :], dt[:, pbi, :], AF.Exp,
                                          scale=Acol[:, pbi, n:n + 1]).then_inc(sem["SE"], 1)

                    @block.vector
                    def _(v, n=n, se_sc0=se_sc0, ve_s0=ve_s0, vea0=vea0,
                          prev_sd=prev_sd):
                        v.wait_ge(sem["VEa"], vea0 + 2)
                        if n == 0:
                            v.wait_ge(sem["VE"], ve_dtu_done)
                        v.tensor_mul(dBu[:, :, :], u_sb[:, :, 0:TC],
                                     bcB[:, 0:1, :].broadcast_to([128, NSTATE, TC])
                                     ).then_inc(sem["VE"], 1)
                        v.wait_ge(sem["VE"], ve_s0 + 1)
                        for pbi in range(NSTATE):
                            v.wait_ge(sem["SE"], se_sc0 + pbi + 1)
                            v.tensor_tensor_scan(dBu[:, pbi, :], dA2[:, pbi % 2, :],
                                                 dBu[:, pbi, :],
                                                 carry[:, pbi, n:n + 1],
                                                 ALU.mult, ALU.add).then_inc(sem["VE"], 1)
                        v.wait_ge(sem["VE"], ve_s0 + NSTATE + 1)
                        v.tensor_copy(carry[:, :, n:n + 1],
                                      dBu[:, :, TC - 1:TC]).then_inc(sem["VE"], 1)
                        v.wait_ge(sem["VE"], ve_s0 + NSTATE + 2)
                        bcCb = bcC[:, 0:1, :].broadcast_to([128, NSTATE, TC])
                        if n == 0:
                            v.tensor_mul(yb[:, :, :], dBu[:, :, :],
                                         bcCb).then_inc(sem["VE"], 1)
                        else:
                            v.tensor_mul(dBu[:, :, :], dBu[:, :, :],
                                         bcCb).then_inc(sem["VE"], 1)
                            v.wait_ge(sem["VE"], ve_s0 + NSTATE + 3)
                            v.tensor_add(yb[:, :, :], yb[:, :, :],
                                         dBu[:, :, :]).then_inc(sem["VE"], 1)
                    prev_scan_done = ve_s0 + (nve if n > 0 else nve - 1)

                se_sz = pb.reserve("SE", 1)

                @block.scalar
                def _(s_, se_z0=se_z0):
                    s_.wait_ge(sem["SE"], se_z0 + 16)
                    s_.wait_ge(sem["VE"], prev_scan_done)
                    s_.activation(zb[:, :, :], zb[:, :, :], AF.Silu).then_inc(sem["SE"], 1)

                ve_g0 = pb.reserve("VE", 17)

                @block.vector
                def _(v, se_sz=se_sz, ve_g0=ve_g0, pe_xd=pe_xd,
                      prev_sd=prev_scan_done):
                    v.wait_ge(sem["SE"], se_sz + 1)
                    v.wait_ge(sem["VE"], prev_sd)
                    v.wait_ge(sem["PE"], pe_xd + 1)
                    for pbi in range(16):
                        v.scalar_tensor_tensor(yb[:, pbi, :], uc[:, pbi, :],
                                               Dcol[:, pbi:pbi + 1], yb[:, pbi, :],
                                               ALU.mult, ALU.add).then_inc(sem["VE"], 1)
                    v.wait_ge(sem["VE"], ve_g0 + 16)
                    v.tensor_mul(ygb[:, :, :], yb[:, :, :],
                                 zb[:, :, :]).then_inc(sem["VE"], 1)
                ve_yg = ve_g0 + 17

                pe_o0 = pb.reserve("PE", 8)
                ve_o0 = pb.reserve("VE", 8)

                @block.tensor
                def _(tn, pe_o0=pe_o0, ve_o0=ve_o0, lg_op=lg_op, ve_yg=ve_yg,
                      se_sp0=se_sp0):
                    tn.wait_ge(sem["LG"], lg_op)
                    tn.wait_ge(sem["VE"], ve_yg)
                    tn.wait_ge(sem["SE"], se_sp0 + 64)
                    for ob in range(8):
                        if ob >= 2:
                            tn.wait_ge(sem["VE"], ve_o0 + ob - 1)
                        for cb in range(16):
                            mi = tn.matmul(psI[ob % 2][:, 0:TC],
                                           Wsh[:, cb * 1024 + ob * 128:
                                               cb * 1024 + (ob + 1) * 128],
                                           ygb[:, cb, :],
                                           start=(cb == 0), stop=(cb == 15))
                        mi.then_inc(sem["PE"], 1)
                pe_op_prev = pe_o0 + 8

                @block.vector
                def _(v, pe_o0=pe_o0):
                    for ob in range(8):
                        v.wait_ge(sem["PE"], pe_o0 + ob + 1)
                        v.scalar_tensor_tensor(XC[:, ob, :], XC[:, ob, :], 2.0,
                                               psI[ob % 2][:, 0:TC],
                                               ALU.mult, ALU.add).then_inc(sem["VE"], 1)
                ve_resid = ve_o0 + 8

                ve_ln = emit_ln(pb, ln_ctx, XC, 8, TC, pre=[("VE", ve_resid)])
                ve_ln_prev = ve_ln

                @block.sync
                def _(sy, si=si, t0=t0, ve_ln=ve_ln):
                    sy.wait_ge(sem["ST"], pb.cnt["ST"])
                    sy.wait_ge(sem["VE"], ve_ln)
                    sy.dma_start(out=hbm[xout_name][si, :, t0:t0 + TC]
                                 .rearrange("(c p) n -> p c n", p=128),
                                 in_=XC[:, :, :]).then_inc(sem["ST"], 16)
                st_x3 = pb.bump("ST", 16)

        @block.sync
        def _(sy):
            sy.wait_ge(sem["ST"], pb.cnt["ST"])


PHASE_EMIT["C"] = emit_phaseC





def emit_phaseD(pb, cfg, hbm, xin_name, xout_name):
    nc, block, sem = pb.nc, pb.block, pb.sem
    S, T, TTD = cfg.S, cfg.T, cfg.TTD
    THN = TTD // 512
    TPS = T // TTD
    NT = S * TPS
    start = pb.snap()
    index = []
    nmat = 0
    for L in range(8):
        for lpb in range(2):
            gset = sorted({((2 ** L - 1) + ((lpb * 128 + m) >> (8 - L))) // 128
                           for m in range(128)})
            ents = []
            for gpb in gset:
                ents.append((gpb, nmat))
                nmat += 1
            index.append((L, lpb, ents))

    with ExitStack() as st:
        x3b = sb(st, nc, "dx3b", [128, 8, TTD], BF16)
        xr = sb(st, nc, "dxr", [128, 8, TTD], F32)
        qb = sb(st, nc, "dqb", [128, 8, TTD], BF16)
        gates = sb(st, nc, "dgates", [128, 2, TTD], F32)
        lp = sb(st, nc, "dlp", [128, 2, TTD], F32)
        lpb_ = sb(st, nc, "dlpb", [128, 2, TTD], BF16)
        retb = sb(st, nc, "dretb", [128, 8, TTD], BF16)
        fsb = sb(st, nc, "dfsb", [128, 2, 512], F32)
        Wq2S = sb(st, nc, "dWq2", [128, 8, D], BF16)
        RtS = sb(st, nc, "dRt", [128, 8, 256], BF16)
        LVS = sb(st, nc, "dLV", [128, 2, D], BF16)
        Wo2S = sb(st, nc, "dWo2", [128, 8, D], BF16)
        pselS = sb(st, nc, "dpsel", [128, nmat, 128], F32)
        sgnS = sb(st, nc, "dsgn", [128, 2, 8], F32)
        biaS = sb(st, nc, "dbia", [128, 2, 8], F32)
        wob = sb(st, nc, "dwob", [128, 8], F32)
        t1 = sb(st, nc, "dt1", [128, 512], F32)
        t2 = sb(st, nc, "dt2", [128, 512], F32)
        rows = sb(st, nc, "drows", [1, 3, 512], F32)
        onec = sb(st, nc, "donec", [128, 1], F32)
        oner = sb(st, nc, "doner", [1, 128], F32)
        epsln = sb(st, nc, "depsln", [1, 1], F32)
        psQ = [ps(st, nc, f"dpsQ{i}", [128, 512]) for i in range(2)]
        psF = [ps(st, nc, f"dpsF{i}", [128, 512]) for i in range(2)]

        ln_ctx = dict(onec=onec, oner=oner, psA=psQ[0], psB=psQ[1],
                      t1=t1, t2=t2, r0=rows[:, 0, :], r1=rows[:, 1, :],
                      r2=rows[:, 2, :], eps_ap=epsln)

        @block.sync
        def _(sy):
            pb.barrier(sy, start)
            sy.wait_ge(sem["LD"], pb.cnt["LD"])
            for dst, nm in ((onec, "ones_col"), (oner, "ones_row"),
                            (epsln, "eps_ln"), (pselS, "psel"),
                            (sgnS, "sgn_tab"), (biaS, "bias_tab"),
                            (wob, "Woutb_fm")):
                sy.dma_start(out=dst[:], in_=hbm[nm]).then_inc(sem["LD"], 16)
        ld_consts = pb.bump("LD", 16 * 7)

        @block.gpsimd
        def _(g):
            pb.barrier(g, start)
            g.wait_ge(sem["LG"], pb.cnt["LG"])
            g.dma_start(out=Wq2S[:], in_=r128(hbm["Wq2T"])).then_inc(sem["LG"], 16)
            g.dma_start(out=RtS[:], in_=r128(hbm["routersT"])).then_inc(sem["LG"], 16)
            g.dma_start(out=LVS[:], in_=r128(hbm["LV"])).then_inc(sem["LG"], 16)
            g.dma_start(out=Wo2S[:], in_=r128(hbm["Wout2T"])).then_inc(sem["LG"], 16)
        lg_w = pb.bump("LG", 16 * 4)

        st_y = 0
        ve_ln_last = 0

        for tile in range(NT):
            si = tile // TPS
            t0 = (tile % TPS) * TTD

            @block.sync
            def _(sy, si=si, t0=t0, st_y=st_y):
                sy.wait_ge(sem["LD"], pb.cnt["LD"])
                if st_y:
                    sy.wait_ge(sem["ST"], st_y)
                if ve_ln_last:
                    sy.wait_ge(sem["VE"], ve_ln_last)
                sy.dma_start(out=xr[:], in_=hbm[xin_name][si, :, t0:t0 + TTD]
                             .rearrange("(c p) n -> p c n", p=128)).then_inc(sem["LD"], 16)
            ld_xr = pb.bump("LD", 16)

            @block.gpsimd
            def _(g, si=si, t0=t0):
                g.wait_ge(sem["LG"], pb.cnt["LG"])
                g.wait_ge(sem["PE"], pb.cnt["PE"])
                g.dma_start(out=x3b[:], in_=hbm[xin_name][si, :, t0:t0 + TTD]
                            .rearrange("(c p) n -> p c n", p=128)).then_inc(sem["LG"], 16)
            lg_x3 = pb.bump("LG", 16)

            pe_q0 = pb.reserve("PE", 8 * THN)
            se_q0 = pb.reserve("SE", 8 * THN)

            @block.tensor
            def _(tn, lg_x3=lg_x3, pe_q0=pe_q0, se_q0=se_q0, tile=tile):
                tn.wait_ge(sem["LG"], lg_x3)
                if tile == 0:
                    tn.wait_ge(sem["LG"], lg_w)
                if ve_ln_last:
                    tn.wait_ge(sem["VE"], ve_ln_last)
                for ob in range(8):
                    for th in range(THN):
                        idx = ob * THN + th
                        if idx >= 2:
                            tn.wait_ge(sem["SE"], se_q0 + idx - 1)
                        for cb in range(8):
                            mi = tn.matmul(psQ[idx % 2][:, :],
                                           Wq2S[:, cb, ob * 128:(ob + 1) * 128],
                                           x3b[:, cb, th * 512:(th + 1) * 512],
                                           start=(cb == 0), stop=(cb == 7))
                        mi.then_inc(sem["PE"], 1)

            @block.scalar
            def _(s_, pe_q0=pe_q0, tile=tile):
                if tile == 0:
                    s_.wait_ge(sem["LD"], ld_consts)
                for idx in range(8 * THN):
                    s_.wait_ge(sem["PE"], pe_q0 + idx + 1)
                    ob, th = idx // THN, idx % THN
                    s_.copy(qb[:, ob, th * 512:(th + 1) * 512],
                            psQ[idx % 2][:, :]).then_inc(sem["SE"], 1)
            se_qb = se_q0 + 8 * THN

            pe_g0 = pb.reserve("PEb", 2 * THN)
            se_g0 = pb.reserve("SE", 2 * THN)

            @block.tensor
            def _(tn, pe_g0=pe_g0, se_g0=se_g0, se_qb=se_qb):
                tn.wait_ge(sem["SE"], se_qb)
                for gpb in range(2):
                    for th in range(THN):
                        idx = gpb * THN + th
                        if idx >= 2:
                            tn.wait_ge(sem["SE"], se_g0 + idx - 1)
                        for cb in range(8):
                            mi = tn.matmul(psF[idx % 2][:, :],
                                           RtS[:, cb, gpb * 128:(gpb + 1) * 128],
                                           qb[:, cb, th * 512:(th + 1) * 512],
                                           start=(cb == 0), stop=(cb == 7))
                        mi.then_inc(sem["PEb"], 1)

            @block.scalar
            def _(s_, pe_g0=pe_g0, se_g0=se_g0):
                for idx in range(2 * THN):
                    s_.wait_ge(sem["PEb"], pe_g0 + idx + 1)
                    gpb, th = idx // THN, idx % THN
                    s_.activation(gates[:, gpb, th * 512:(th + 1) * 512],
                                  psF[idx % 2][:, :], AF.Sigmoid).then_inc(sem["SE"], 1)
            se_gates = se_g0 + 2 * THN

            nstep = 0
            pe_f0 = pb.reserve("PEb", 16 * THN)
            se_f0 = pb.reserve("SE", 16 * THN)
            ve_f0 = pb.reserve("VE", 16 * THN)

            @block.tensor
            def _(tn, pe_f0=pe_f0, se_f0=se_f0, se_gates=se_gates, ve_f0=ve_f0):
                tn.wait_ge(sem["SE"], se_gates)
                k = 0
                for (L, lpb, ents) in index:
                    for th in range(THN):
                        if k >= 2:
                            tn.wait_ge(sem["SE"], se_f0 + k - 1)
                        for j, (gpb, mi_) in enumerate(ents):
                            mm = tn.matmul(psF[k % 2][:, :], pselS[:, mi_, :],
                                           gates[:, gpb, th * 512:(th + 1) * 512],
                                           start=(j == 0), stop=(j == len(ents) - 1))
                        mm.then_inc(sem["PEb"], 1)
                        k += 1

            @block.scalar
            def _(s_, pe_f0=pe_f0, se_f0=se_f0, ve_f0=ve_f0):
                k = 0
                for (L, lpb, ents) in index:
                    for th in range(THN):
                        s_.wait_ge(sem["PEb"], pe_f0 + k + 1)
                        if k >= 2:
                            s_.wait_ge(sem["VE"], ve_f0 + k - 1)
                        s_.activation(fsb[:, k % 2, :], psF[k % 2][:, :],
                                      AF.Identity, scale=sgnS[:, lpb, L:L + 1],
                                      bias=biaS[:, lpb, L:L + 1]).then_inc(sem["SE"], 1)
                        k += 1

            @block.vector
            def _(v, se_f0=se_f0, ve_f0=ve_f0):
                k = 0
                for (L, lpb, ents) in index:
                    for th in range(THN):
                        v.wait_ge(sem["SE"], se_f0 + k + 1)
                        sl = lp[:, lpb, th * 512:(th + 1) * 512]
                        if L == 0:
                            v.tensor_copy(sl, fsb[:, k % 2, :]).then_inc(sem["VE"], 1)
                        else:
                            if k >= 2 * THN:
                                v.wait_ge(sem["VE"], ve_f0 + k - 2 * THN + 1)
                            v.tensor_mul(sl, sl, fsb[:, k % 2, :]).then_inc(sem["VE"], 1)
                        k += 1
            ve_lp = ve_f0 + 16 * THN

            ve_lpb = pb.reserve("VE", 1)

            @block.vector
            def _(v, ve_lp=ve_lp):
                v.wait_ge(sem["VE"], ve_lp)
                v.tensor_copy(lpb_[:, :, :], lp[:, :, :]).then_inc(sem["VE"], 1)

            pe_r0 = pb.reserve("PE", 8 * THN)
            se_r0 = pb.reserve("SE", 8 * THN)

            @block.tensor
            def _(tn, pe_r0=pe_r0, se_r0=se_r0, ve_lpb=ve_lpb, se_qb=se_qb):
                tn.wait_ge(sem["VE"], ve_lpb + 1)
                for ob in range(8):
                    for th in range(THN):
                        idx = ob * THN + th
                        if idx >= 2:
                            tn.wait_ge(sem["SE"], se_r0 + idx - 1)
                        for cb in range(2):
                            mi = tn.matmul(psQ[idx % 2][:, :],
                                           LVS[:, cb, ob * 128:(ob + 1) * 128],
                                           lpb_[:, cb, th * 512:(th + 1) * 512],
                                           start=(cb == 0), stop=(cb == 1))
                        mi.then_inc(sem["PE"], 1)

            @block.scalar
            def _(s_, pe_r0=pe_r0):
                for idx in range(8 * THN):
                    s_.wait_ge(sem["PE"], pe_r0 + idx + 1)
                    ob, th = idx // THN, idx % THN
                    s_.copy(retb[:, ob, th * 512:(th + 1) * 512],
                            psQ[idx % 2][:, :]).then_inc(sem["SE"], 1)
            se_ret = se_r0 + 8 * THN

            pe_w0 = pb.reserve("PEb", 8 * THN)
            ve_w0 = pb.reserve("VE", 8 * THN)

            @block.tensor
            def _(tn, pe_w0=pe_w0, ve_w0=ve_w0, se_ret=se_ret):
                tn.wait_ge(sem["SE"], se_ret)
                for ob in range(8):
                    for th in range(THN):
                        idx = ob * THN + th
                        if idx >= 2:
                            tn.wait_ge(sem["VE"], ve_w0 + idx - 1)
                        for cb in range(8):
                            mi = tn.matmul(psF[idx % 2][:, :],
                                           Wo2S[:, cb, ob * 128:(ob + 1) * 128],
                                           retb[:, cb, th * 512:(th + 1) * 512],
                                           start=(cb == 0), stop=(cb == 7))
                        mi.then_inc(sem["PEb"], 1)

            @block.vector
            def _(v, pe_w0=pe_w0, ld_xr=ld_xr):
                v.wait_ge(sem["LD"], ld_xr)
                for ob in range(8):
                    for th in range(THN):
                        idx = ob * THN + th
                        v.wait_ge(sem["PEb"], pe_w0 + idx + 1)
                        v.scalar_tensor_tensor(xr[:, ob, th * 512:(th + 1) * 512],
                                               psF[idx % 2][:, :], wob[:, ob:ob + 1],
                                               xr[:, ob, th * 512:(th + 1) * 512],
                                               ALU.add, ALU.add).then_inc(sem["VE"], 1)
            ve_out = ve_w0 + 8 * THN

            ve_lns = []
            ve_pre = ve_out
            for th in range(THN):
                xslice = xr[:, :, th * 512:(th + 1) * 512]
                ve_l = emit_ln(pb, ln_ctx, xslice, 8, 512,
                               pre=[("VE", ve_pre)])
                ve_lns.append(ve_l)
                ve_pre = ve_l
            ve_ln_last = ve_lns[-1]

            @block.sync
            def _(sy, si=si, t0=t0, ve_lns=ve_lns):
                sy.wait_ge(sem["ST"], pb.cnt["ST"])
                for vv in ve_lns:
                    sy.wait_ge(sem["VE"], vv)
                sy.dma_start(out=hbm[xout_name][si, :, t0:t0 + TTD]
                             .rearrange("(c p) n -> p c n", p=128),
                             in_=xr[:, :, :]).then_inc(sem["ST"], 16)
            st_y = pb.bump("ST", 16)

        @block.sync
        def _(sy):
            sy.wait_ge(sem["ST"], pb.cnt["ST"])


PHASE_EMIT["D"] = emit_phaseD



PHASE_IO = {"A": ("x", "x1"), "B": ("x1", "x2"), "C": ("x2", "x3"), "D": ("x3", "y")}


def build_program(cfg, phases, arrs):
    nc = bass.Bass("TRN2", target_bir_lowering=False)
    hbm = {}
    S, T = cfg.S, cfg.T
    first_in = PHASE_IO[phases[0]][0]
    last_out = PHASE_IO[phases[-1]][1]
    act_names = [first_in] + [PHASE_IO[p][1] for p in phases]
    for nm in act_names:
        if nm == first_in:
            kind = "ExternalInput"
        elif nm == last_out:
            kind = "ExternalOutput"
        else:
            kind = "Internal"
        hbm[nm] = nc.dram_tensor(nm, [S, D, T], F32, kind=kind).ap()
    if "C" in phases:
        hbm["z_spill"] = nc.dram_tensor("z_spill", [S, 16, 128, T], F32,
                                        kind="Internal").ap()
    for name, a in arrs.items():
        if name.startswith("_"):
            continue
        hbm[name] = nc.dram_tensor(name, list(a.shape), F32,
                                   kind="ExternalInput").ap()
    with ExitStack() as stack:
        pb = PB(nc, stack, None)
        block = stack.enter_context(nc.Block())
        pb.block = block
        for p in phases:
            PHASE_EMIT[p](pb, cfg, hbm, PHASE_IO[p][0], PHASE_IO[p][1])
    return nc


PHASE_EMIT["A"] = emit_phaseA


def make_arrs(inputs, cfg):
    consts = make_consts(cfg)
    w = prep_weights(inputs)
    arrs = {k: v for k, v in consts.items() if not k.startswith("_")}
    arrs.update(w)
    return arrs, consts["_psel_index"]


KERNEL_GROUPS = ["ABCD"]


def kernel(**inputs):
    _ = inputs.get("L1_ffn_leaf_keys")
    cfg = Cfg()
    arrs, psel_index = make_arrs(inputs, cfg)
    x = inputs["L0_attn_x"]
    B = x.shape[0]
    shards = [np.ascontiguousarray(x[2 * c:2 * c + 2].transpose(0, 2, 1))
              for c in range(N_CORES)]
    y = np.empty((B, cfg.T, D), np.float32)
    cur = shards
    for grp in KERNEL_GROUPS:
        nc = build_program(cfg, grp, arrs)
        iname = PHASE_IO[grp[0]][0]
        oname = PHASE_IO[grp[-1]][1]
        in_maps = [dict(arrs, **{iname: cur[c]}) for c in range(N_CORES)]
        res = run_bass_kernel_spmd(nc, in_maps, core_ids=list(range(N_CORES)))
        cur = [np.asarray(r[oname]) for r in res.results]
    for c in range(N_CORES):
        y[2 * c:2 * c + 2] = cur[c].transpose(0, 2, 1)
    return y
